# version 04
# baseline (speedup 1.0000x reference)
import math
from contextlib import ExitStack

import numpy as np
import concourse.bass as bass
import concourse.mybir as mybir
from concourse.bass_utils import run_bass_kernel_spmd

F32 = mybir.dt.float32
BF16 = mybir.dt.bfloat16
U32 = mybir.dt.uint32
AF = mybir.ActivationFunctionType
ALU = mybir.AluOpType
AX = mybir.AxisListType

P = 128
D = 2048
NDC = D // P
TOWN = 1024
NT = TOWN // P
SG = 2048
NTG = SG // P
NE = 16
FI = 1408
NIC = FI // P
SFI = 2816
SNIC = SFI // P
INV_SQRT128 = 1.0 / math.sqrt(128.0)

ENGS = ("t", "v", "s", "y", "g")
NOW = ("mT", 0)
DBG_STOP = None
BARRIER_ENGS = ("t", "v", "s", "y", "g")


class KB:
    def __init__(self, nc):
        self.nc = nc
        self.q = {e: [] for e in ENGS}
        self.sems = {}
        self.cnt = {}
        self.hw = {e: {} for e in ENGS}

    def sem_init(self, es, names):
        for n in names:
            self.sems[n] = es.enter_context(self.nc.semaphore(n))
            self.cnt[n] = 0

    def emit(self, e, fn):
        self.q[e].append(fn)

    def wait(self, e, sn, val):
        if val <= self.hw[e].get(sn, 0):
            return
        self.hw[e][sn] = val
        h = self.sems[sn]
        self.q[e].append(lambda E, h=h, v=val: E.wait_ge(h, v))

    def waits(self, e, lst):
        for w in lst:
            if w:
                self.wait(e, w[0], w[1])

    def inc(self, sn, n=1):
        self.cnt[sn] += n
        return self.cnt[sn]

    def dma(self, e, out_ap, in_ap, waits=(), stream=None, **kw):
        sn = stream if stream else ("dG" if e == "g" else "dY")
        self.waits(e, waits)
        if stream:
            self.wait(e, sn, self.cnt[sn])
        h = self.sems[sn]
        v = self.inc(sn, 16)
        self.emit(e, lambda E, o=out_ap, i=in_ap, h=h, kw=dict(kw):
                  E.dma_start(out=o, in_=i, **kw).then_inc(h, 16))
        return (sn, v)

    def mm1(self, psum_ap, lhsT, rhs, start, stop, waits=(), mark=False):
        self.waits("t", waits)
        if mark:
            h, v = self.sems["mT"], self.inc("mT")
            self.emit("t", lambda E, ps=psum_ap, l=lhsT, r=rhs, st=start, sp=stop, h=h:
                      E.matmul(ps, l, r, start=st, stop=sp).then_inc(h, 1))
            return ("mT", v)
        self.emit("t", lambda E, ps=psum_ap, l=lhsT, r=rhs, st=start, sp=stop:
                  E.matmul(ps, l, r, start=st, stop=sp))
        return None

    def mmg(self, psum_ap, pairs, waits=()):
        self.waits("t", waits)
        h, v = self.sems["mT"], self.inc("mT")
        n = len(pairs)

        def f(E, ps=psum_ap, pairs=list(pairs), h=h, n=n):
            for i, (l, r) in enumerate(pairs):
                mm = E.matmul(ps, l, r, start=(i == 0), stop=(i == n - 1))
            mm.then_inc(h, 1)
        self.emit("t", f)
        return ("mT", v)

    def transpose_pack(self, out_list, in_list, ident, waits=()):
        self.waits("t", waits)
        h, v = self.sems["mT"], self.inc("mT")

        def f(E, outs=list(out_list), ins=list(in_list), ident=ident, h=h):
            for o, i in zip(outs, ins):
                tr = E.transpose(o, i, ident)
            tr.then_inc(h, 1)
        self.emit("t", f)
        return ("mT", v)

    def act(self, out_ap, in_ap, func, waits=(), bias=None, scale=None, accum=None):
        self.waits("s", waits)
        h, v = self.sems["mS"], self.inc("mS")

        def f(E, o=out_ap, i=in_ap, func=func, bias=bias, scale=scale, accum=accum, h=h):
            kw = {}
            if bias is not None:
                kw["bias"] = bias
            if scale is not None:
                kw["scale"] = scale
            if accum is not None:
                kw["accum_out"] = accum
            E.activation(o, i, func, **kw).then_inc(h, 1)
        self.emit("s", f)
        return ("mS", v)

    def scopy(self, out_ap, in_ap, waits=()):
        self.waits("s", waits)
        h, v = self.sems["mS"], self.inc("mS")
        self.emit("s", lambda E, o=out_ap, i=in_ap, h=h: E.copy(o, i).then_inc(h, 1))
        return ("mS", v)

    def vop(self, name, args, waits=(), kwargs=None):
        self.waits("v", waits)
        h, v = self.sems["mV"], self.inc("mV")
        kwargs = dict(kwargs or {})

        def f(E, name=name, args=tuple(args), kwargs=kwargs, h=h):
            getattr(E, name)(*args, **kwargs).then_inc(h, 1)
        self.emit("v", f)
        return ("mV", v)

    def gswitch(self, body, waits=()):
        self.waits("g", waits)
        nc = self.nc

        def f(E, body=body):
            for arm in nc.switch_core_id([E], n=128):
                b = arm.value if hasattr(arm, "value") else int(arm)
                body(E, (b % 8) % 2)
        self.emit("g", f)

    def gpoll(self, flag_ap, fvals_sb, k, tag):
        def f(E, k=k, tag=tag):
            with E.register(f"pa{tag}") as ra, E.register(f"pn{tag}") as rn:
                E.load(rn, fvals_sb[0:1, k:k + 1])
                for slot in (0, 1):
                    E.reg_mov(ra, 1)
                    with E.While(ra):
                        E.load(ra, flag_ap[k:k + 1, slot:slot + 1])
                        E.reg_alu(ra, ra, rn, ALU.not_equal)
        self.emit("g", f)

    def barrier(self):
        marks = [("t", "mT", self.cnt["mT"]), ("v", "mV", self.cnt["mV"]),
                 ("s", "mS", self.cnt["mS"]), ("y", "dY", self.cnt["dY"]),
                 ("g", "dG", self.cnt["dG"])]
        for e in ENGS:
            for e2, sn, v in marks:
                if e2 != e:
                    self.wait(e, sn, v)
        for sn in ("bc", "xl", "wA", "wP", "w130", "w131", "w20", "w21", "rX0", "rX1"):
            for e in ENGS:
                self.wait(e, sn, self.cnt[sn])


def build_program(phases):
    nc = bass.Bass("TRN2", target_bir_lowering=False)
    ap = {}

    def din(name, shape, dtype=F32):
        ap[name] = nc.dram_tensor(name, shape, dtype, kind="ExternalInput").ap()

    din("x_own", [TOWN, D])
    din("x_full", [SG, D])
    din("qkvT", [D, 1536])
    din("projT", [D, D])
    din("wq", [D, 1024])
    din("wkv", [D, 256])
    din("wout", [D, D])
    din("gn", [D])
    din("go", [D])
    need_f = ("f1" in phases) or ("f2" in phases)
    din("gwT", [D, NE])
    if need_f:
        din("w13T", [NE, D, SFI])
        din("w2T", [NE, FI, D])
        din("sw13T", [D, 2 * SFI])
        din("sw2T", [SFI, D])
    din("ident", [P, P])
    din("ones", [P, P])
    din("masks", [4, P, 512])
    din("fvals", [1, 8], U32)
    din("bvals", [1, 8])
    out_ap = nc.dram_tensor("out", [TOWN, D], F32, kind="ExternalOutput").ap()
    dbg_ap = nc.dram_tensor("dbg", [P, 32768], F32, kind="ExternalOutput").ap()

    sh_xn = nc.dram_tensor("sh_xn", [SG, D], BF16, addr_space="Shared").ap()
    sh_oh = {"a": nc.dram_tensor("sh_oha", [16, P, SG], BF16, addr_space="Shared").ap(),
             "b": nc.dram_tensor("sh_ohb", [16, P, SG], BF16, addr_space="Shared").ap()}
    sh_flag = nc.dram_tensor("sh_flag", [8, 2], U32, addr_space="Shared").ap()

    es = ExitStack()
    with es:
        kb = KB(nc)
        kb.sem_init(es, ["dG", "dY", "mT", "mV", "mS", "bc", "xl", "wA", "wP",
                         "w130", "w131", "w20", "w21", "rX0", "rX1"])

        R = es.enter_context(nc.sbuf_tensor("R", [P, NT, D], F32))
        ident_bf = es.enter_context(nc.sbuf_tensor("ident_bf", [P, P], BF16))
        ones_bf = es.enter_context(nc.sbuf_tensor("ones_bf", [P, P], BF16))
        masks_bf = es.enter_context(nc.sbuf_tensor("masks_bf", [P, 4, 512], BF16))
        gwT_sb = es.enter_context(nc.sbuf_tensor("gwT_sb", [P, NDC, NE], BF16))
        c_sb = es.enter_context(nc.sbuf_tensor("c_sb", [P, NT, NE], F32))
        fvals_sb = es.enter_context(nc.sbuf_tensor("fvals_sb", [1, 8], U32))
        sml = es.enter_context(nc.sbuf_tensor("sml", [P, 48], F32))
        bv = es.enter_context(nc.sbuf_tensor("bv", [P, 8], F32))
        gate_t = es.enter_context(nc.sbuf_tensor("gate_t", [P, 64], F32))

        kb.dma("g", ident_bf[:, :], ap["ident"][:, :], stream="bc")
        kb.dma("g", ones_bf[:, :], ap["ones"][:, :], stream="bc")
        kb.dma("g", masks_bf[:, :, :], ap["masks"].rearrange("m p q -> p m q"), stream="bc")
        kb.dma("g", gwT_sb[:, :, :], ap["gwT"].rearrange("(c p) e -> p c e", p=P), stream="bc")
        kb.dma("g", bv[:, :], ap["bvals"][0, None, :].broadcast_to((P, 8)), stream="bc")
        c_consts = kb.dma("g", fvals_sb[:, :], ap["fvals"][:, :], stream="bc")
        c_rinit = kb.dma("y", R[:, :, :], ap["x_own"].rearrange("(a p) d -> p a d", p=P),
                         stream="xl")
        for e in ("t", "v", "s", "g"):
            kb.wait(e, c_consts[0], c_consts[1])
        for e in ("t", "v", "s"):
            kb.wait(e, c_rinit[0], c_rinit[1])

        def dbg_dump(src_ap, nelem, waits=()):
            kb.dma("g", dbg_ap[:, 0:nelem], src_ap, waits=list(waits))

        def maybe_stop(kind, stage, buf_ap=None, nelem=0, waits=()):
            if DBG_STOP == (kind, stage):
                if buf_ap is not None:
                    dbg_dump(buf_ap, nelem, waits)
                return True
            return False

        def ln_stats(src_ap, junk_ap, col, src_ready=()):
            msum = sml[:, 4 * col + 0:4 * col + 1]
            s_x = sml[:, 4 * col + 1:4 * col + 2]
            mu = sml[:, 4 * col + 2:4 * col + 3]
            rstd = sml[:, 4 * col + 3:4 * col + 4]
            cS = kb.act(junk_ap, src_ap, AF.Square, waits=list(src_ready), accum=msum)
            cV1 = kb.vop("tensor_reduce", [s_x, src_ap, AX.X, ALU.add], waits=list(src_ready))
            cV2 = kb.vop("tensor_scalar", [mu, s_x, 1.0 / D, None],
                         kwargs={"op0": ALU.mult}, waits=[cV1])
            cV3 = kb.vop("tensor_scalar", [msum, msum, 1.0 / D, None],
                         kwargs={"op0": ALU.mult}, waits=[cS])
            cV4 = kb.vop("scalar_tensor_tensor", [msum, mu, mu, msum],
                         kwargs={"op0": ALU.mult, "op1": ALU.subtract}, waits=[cV2, cV3])
            cS2 = kb.act(rstd, msum, AF.Ln, waits=[cV4], bias=bv[:, 0:1], scale=-1.0)
            cS3 = kb.act(rstd, rstd, AF.Exp, waits=[cS2], scale=-0.5)
            cV5 = kb.vop("tensor_tensor", [s_x, mu, rstd], kwargs={"op": ALU.mult}, waits=[cS3])
            return rstd, s_x, cV5

        def ln_to_xn(src_ap, out_ap, junk_ap, col, gamma_rep=None, tmp_f32=None,
                     src_ready=(), out_free=(), tmp_free=()):
            rstd, murstd, c_st = ln_stats(src_ap, junk_ap, col, src_ready)
            if gamma_rep is None:
                return kb.vop("tensor_scalar", [out_ap, src_ap, rstd, murstd],
                              kwargs={"op0": ALU.mult, "op1": ALU.subtract},
                              waits=[c_st] + list(out_free))
            cV = kb.vop("tensor_scalar", [tmp_f32, src_ap, rstd, murstd],
                        kwargs={"op0": ALU.mult, "op1": ALU.subtract},
                        waits=[c_st] + list(tmp_free))
            return kb.vop("tensor_tensor", [out_ap, tmp_f32, gamma_rep],
                          kwargs={"op": ALU.mult}, waits=[cV] + list(out_free))

        def attn_phase(kind):
            causal = kind == "a"
            xfl = 0 if kind == "a" else 1
            ofl = 2 if kind == "a" else 3
            shoh = sh_oh[kind]
            pes = ExitStack()
            with pes:
                xnf = pes.enter_context(nc.sbuf_tensor(f"xnf_{kind}", [P, NDC, SG], BF16))
                xg = [pes.enter_context(nc.sbuf_tensor(f"xg_{kind}{i}", [P, D], BF16)) for i in range(2)]
                wbufA = pes.enter_context(nc.sbuf_tensor(f"wbufA_{kind}", [P, NDC, 256], BF16))
                sc_sb = pes.enter_context(nc.sbuf_tensor(f"scs_{kind}", [P, 2, SG], F32))
                ngr = 2 if kind == "a" else 1
                k_fm = pes.enter_context(nc.sbuf_tensor(f"kfm_{kind}", [P, ngr, SG], BF16))
                v_tm = pes.enter_context(nc.sbuf_tensor(f"vtm_{kind}", [P, NTG, 128 * ngr], BF16))
                q_bf = [pes.enter_context(nc.sbuf_tensor(f"qbf_{kind}{i}", [P, SG], BF16)) for i in range(2)]
                ohb = pes.enter_context(nc.sbuf_tensor(f"oh_{kind}", [P, SG], BF16))
                exp_sb = pes.enter_context(nc.sbuf_tensor(f"exps_{kind}", [P, 2, SG], BF16))
                junk = exp_sb[:, 0, :].tensor if False else None
                attn_T = pes.enter_context(nc.sbuf_tensor(f"attnT_{kind}", [P, NTG, P], BF16))
                if kind == "b":
                    rtmp = [pes.enter_context(nc.sbuf_tensor(f"rtmp_{kind}", [P, 512], F32))]
                xtmpf = sc_sb[:, 0, :]
                if kind == "b":
                    gn_rep = pes.enter_context(nc.sbuf_tensor("gn_rep", [P, D], BF16))
                    ktmp = pes.enter_context(nc.sbuf_tensor("ktmp", [P, P], BF16))
                    q2t = pes.enter_context(nc.sbuf_tensor("q2t", [P, 512], BF16))
                pSc = [pes.enter_context(nc.psum_tensor(f"pSc_{kind}{i}", [P, 512], F32))
                       for i in range(2)]
                pq = [pes.enter_context(nc.psum_tensor(f"pq_{kind}{i}", [P, 512], F32))
                      for i in range(2)]
                pD = pes.enter_context(nc.psum_tensor(f"pD_{kind}", [P, 512], F32))
                pAV = pes.enter_context(nc.psum_tensor(f"pAV_{kind}", [P, 512], F32))
                ptr = pes.enter_context(nc.psum_tensor(f"ptr_{kind}", [P, 1024], BF16))

                rel = {}

                def rget(key):
                    return rel.get(key, NOW)

                if kind == "b":
                    c_gn = kb.dma("g", gn_rep[:, :], ap["gn"][None, :].broadcast_to((P, D)), stream="bc")
                    kb.wait("v", c_gn[0], c_gn[1])

                if kind == "a":
                    for gt in range(NTG):
                        par = gt % 2
                        c_ld = kb.dma("y", xtmpf, ap["x_full"][gt * P:(gt + 1) * P, :],
                                      waits=[rget("xtmpf")], stream="xl")
                        for e in ("v", "s"):
                            kb.wait(e, c_ld[0], c_ld[1])
                        c_xn = ln_to_xn(xtmpf, xg[par][:, :], exp_sb[:, 0, :], col=par,
                                        src_ready=[c_ld], out_free=[rget(("xg", par))])
                        rel["xtmpf"] = c_xn
                        for half in (0, 1):
                            ins = [xg[par][:, (8 * half + j) * P:(8 * half + j + 1) * P] for j in range(8)]
                            outs = [ptr[:, j * P:(j + 1) * P] for j in range(8)]
                            c_tr = kb.transpose_pack(outs, ins, ident_bf[:, :],
                                                     waits=[c_xn, rget("ptr")])
                            c_ev = kb.vop("tensor_copy",
                                          [xnf[:, 8 * half:8 * half + 8, gt * P:(gt + 1) * P], ptr[:, :]],
                                          waits=[c_tr])
                            rel["ptr"] = c_ev
                        rel[("xg", par)] = c_tr
                    c_xnf = c_ev
                else:
                    for tt in range(NT):
                        par = tt % 2
                        c_xn = ln_to_xn(R[:, tt, :], xg[par][:, :], exp_sb[:, 0, :], col=par,
                                        gamma_rep=gn_rep[:, :], tmp_f32=xtmpf,
                                        tmp_free=[rget("xtmpf")],
                                        out_free=[rget(("xg", par))])
                        rel["xtmpf"] = c_xn
                        kb.wait("g", "mV", c_xn[1])
                        hdG = kb.sems["dG"]
                        v1 = kb.inc("dG", 16)

                        def wr_xn(E, par_, tt=tt, par=par, hdG=hdG):
                            E.dma_start(
                                out=sh_xn[par_ * TOWN + tt * P:par_ * TOWN + (tt + 1) * P, :],
                                in_=xg[par][:, :]).then_inc(hdG, 16)
                        kb.gswitch(wr_xn)
                        rel[("xg", par)] = ("dG", v1)
                    kb.wait("g", "dG", kb.cnt["dG"])
                    v2 = kb.inc("dG", 16)
                    hdG = kb.sems["dG"]

                    def wr_xfl(E, par_, hdG=hdG, xfl=xfl):
                        E.dma_start(out=sh_flag[xfl:xfl + 1, par_:par_ + 1],
                                    in_=fvals_sb[:, xfl:xfl + 1]).then_inc(hdG, 16)
                    kb.gswitch(wr_xfl)
                    kb.hw["g"]["dG"] = max(kb.hw["g"].get("dG", 0), v2)
                    kb.gpoll(sh_flag, fvals_sb, xfl, f"x{kind}")
                    for gt in range(NTG):
                        par = gt % 2
                        c_ld = kb.dma("g", xg[par][:, :], sh_xn[gt * P:(gt + 1) * P, :],
                                      waits=[rget(("xg", par))], stream=f"rX{par}")
                        for half in (0, 1):
                            ins = [xg[par][:, (8 * half + j) * P:(8 * half + j + 1) * P] for j in range(8)]
                            outs = [ptr[:, j * P:(j + 1) * P] for j in range(8)]
                            c_tr = kb.transpose_pack(outs, ins, ident_bf[:, :],
                                                     waits=[c_ld, rget("ptr")])
                            c_ev = kb.vop("tensor_copy",
                                          [xnf[:, 8 * half:8 * half + 8, gt * P:(gt + 1) * P], ptr[:, :]],
                                          waits=[c_tr])
                            rel["ptr"] = c_ev
                        rel[("xg", par)] = c_tr
                    c_xnf = c_ev

                if maybe_stop(kind, "xnf", xnf[:, :, :], NDC * SG, waits=[c_xnf]):
                    return True
                if kind == "a":
                    c_w = kb.dma("g", wbufA[:, :, :],
                                 ap["qkvT"][:, 1024:1280].rearrange("(c p) o -> p c o", p=P),
                                 stream="wA")
                    for gl in range(2):
                        for qs in range(4):
                            pairs = [(wbufA[:, dc, gl * P:(gl + 1) * P],
                                      xnf[:, dc, qs * 512:(qs + 1) * 512]) for dc in range(NDC)]
                            c_mm = kb.mmg(pD[:, :], pairs, waits=[c_w, c_xnf, rget("pkv")])
                            c_k = kb.scopy(k_fm[:, gl, qs * 512:(qs + 1) * 512], pD[:, :], waits=[c_mm])
                            rel["pkv"] = c_k
                    c_w = kb.dma("g", wbufA[:, :, :],
                                 ap["qkvT"][:, 1280:1536].rearrange("(c p) o -> p c o", p=P),
                                 waits=[("mT", kb.cnt["mT"])], stream="wA")
                    for gt in range(NTG):
                        bank = pAV if gt % 2 == 0 else pq[0]
                        pairs = [(xnf[:, dc, gt * P:(gt + 1) * P], wbufA[:, dc, :])
                                 for dc in range(NDC)]
                        c_mm = kb.mmg(bank[:, 0:256], pairs, waits=[c_w, c_xnf, rget(("pS", gt % 2))])
                        c_v = kb.vop("tensor_copy", [v_tm[:, gt, :], bank[:, 0:256]], waits=[c_mm])
                        rel[("pS", gt % 2)] = c_v
                else:
                    c_w = kb.dma("g", wbufA[:, :, 0:256],
                                 ap["wkv"].rearrange("(c p) o -> p c o", p=P), stream="wA")
                    ln4 = math.log(4.0)
                    for gt in range(NTG):
                        pairs = [(xnf[:, dc, gt * P:(gt + 1) * P], wbufA[:, dc, 0:256])
                                 for dc in range(NDC)]
                        c_mm = kb.mmg(pD[:, 0:256], pairs, waits=[c_w, c_xnf, rget("pkv")])
                        ksum = sml[:, 16:17]
                        krs = sml[:, 17:18]
                        cq = kb.act(exp_sb[:, 0, 0:P], pD[:, 0:P], AF.Square, waits=[c_mm], accum=ksum)
                        cl = kb.act(krs, ksum, AF.Ln, waits=[cq], bias=bv[:, 1:2])
                        ce = kb.act(krs, krs, AF.Exp, waits=[cl], scale=-0.5, bias=bv[:, 4:5])
                        ck = kb.vop("tensor_scalar", [ktmp[:, :], pD[:, 0:P], krs, None],
                                    kwargs={"op0": ALU.mult}, waits=[ce, rget("ktmp")])
                        cv = kb.vop("tensor_copy", [v_tm[:, gt, 0:P], pD[:, P:256]], waits=[c_mm])
                        rel["pkv"] = cv
                        c_tr = kb.transpose_pack([ptr[:, 0:P]], [ktmp[:, :]], ident_bf[:, :],
                                                 waits=[ck, rget("ptr")])
                        c_kf = kb.vop("tensor_copy", [k_fm[:, 0, gt * P:(gt + 1) * P], ptr[:, 0:P]],
                                      waits=[c_tr])
                        rel["ptr"] = c_kf
                        rel["ktmp"] = c_tr
                kv_done = ("mT", kb.cnt["mT"])
                if maybe_stop(kind, "kv", k_fm[:, :, :], ngr * SG,
                              waits=[("mV", kb.cnt["mV"]), ("mS", kb.cnt["mS"])]):
                    return True
                if DBG_STOP == (kind, "vv"):
                    dbg_dump(v_tm[:, :, :], NTG * 128 * ngr, waits=[("mV", kb.cnt["mV"])])
                    return True

                exp_scale = INV_SQRT128 if causal else 1.0
                ln64 = math.log(64.0)
                wsrc = ap["qkvT"] if kind == "a" else ap["wq"]
                c_wq = None
                for hl in range(8):
                    par = hl % 2
                    gl = hl // 4 if kind == "a" else 0
                    if hl % 2 == 0:
                        blk = hl // 2
                        c_wq = kb.dma("g", wbufA[:, :, :],
                                      wsrc[:, blk * 256:(blk + 1) * 256].rearrange("(c p) o -> p c o", p=P),
                                      waits=[("mT", kb.cnt["mT"])], stream="wA")
                    col0 = (hl % 2) * P
                    for qs in range(4):
                        bank = pq[qs % 2]
                        pairs = [(wbufA[:, dc, col0:col0 + P], xnf[:, dc, qs * 512:(qs + 1) * 512])
                                 for dc in range(NDC)]
                        c_mm = kb.mmg(bank[:, :], pairs,
                                      waits=[c_wq, c_xnf, rget(("pq", qs % 2))])
                        if kind == "a":
                            c_q = kb.scopy(q_bf[par][:, qs * 512:(qs + 1) * 512], bank[:, :],
                                           waits=[c_mm, rget(("qfree", par))])
                            rel[("pq", qs % 2)] = c_q
                        else:
                            cq2 = kb.act(q2t[:, :], bank[:, :], AF.Square,
                                         waits=[c_mm, rget("q2t")])
                            c_den = kb.mmg(pD[:, :], [(ones_bf[:, :], q2t[:, :])],
                                           waits=[cq2, rget("pD")])
                            rel["q2t"] = c_den
                            c_l = kb.act(rtmp[0][:, :], pD[:, :], AF.Ln,
                                         waits=[c_den, rget(("rt", 0))], scale=256.0, bias=bv[:, 1:2])
                            c_e = kb.act(rtmp[0][:, :], rtmp[0][:, :], AF.Exp, waits=[c_l],
                                         scale=-0.5, bias=bv[:, 5:6])
                            rel["pD"] = c_e
                            c_q = kb.vop("tensor_tensor",
                                         [q_bf[par][:, qs * 512:(qs + 1) * 512], bank[:, :], rtmp[0][:, :]],
                                         kwargs={"op": ALU.mult},
                                         waits=[c_e, rget(("qfree", par))])
                            rel[("pq", qs % 2)] = c_q
                            rel[("rt", 0)] = c_q
                    q_done = c_q
                    m_run = sml[:, 16:17]
                    bm = sml[:, 17:18]
                    dparts = sml[:, 18:22]
                    den = sml[:, 22:23]
                    rcpd = sml[:, 23:24]
                    pend = None

                    def softmax_stage(qt):
                        epar = qt % 2
                        n_ks = (qt // 4 + 1) if causal else 4
                        nkc = (qt + 1) if causal else NTG
                        mb = 24 + 12 * epar
                        m_run = sml[:, mb:mb + 1]
                        bm = sml[:, mb + 1:mb + 2]
                        dparts = sml[:, mb + 2:mb + 6]
                        den = sml[:, mb + 6:mb + 7]
                        rcpd = sml[:, mb + 7:mb + 8]
                        mparts = sml[:, mb + 8:mb + 12]
                        c_pm = []
                        for ks in range(n_ks):
                            bank = pSc[ks % 2]
                            c1 = kb.mmg(bank[:, :],
                                        [(q_bf[par][:, qt * P:(qt + 1) * P],
                                          k_fm[:, gl, ks * 512:(ks + 1) * 512])],
                                        waits=[q_done, kv_done, rget(("psc", ks % 2))])
                            c2 = kb.scopy(sc_sb[:, epar, ks * 512:(ks + 1) * 512], bank[:, :],
                                          waits=[c1, rget(("scs", epar))])
                            rel[("psc", ks % 2)] = c2
                            diag = causal and ks == qt // 4
                            if diag:
                                c2 = kb.vop("tensor_tensor",
                                            [sc_sb[:, epar, ks * 512:(ks + 1) * 512],
                                             sc_sb[:, epar, ks * 512:(ks + 1) * 512],
                                             masks_bf[:, qt % 4, :]],
                                            kwargs={"op": ALU.add}, waits=[c2])
                            c3 = kb.vop("tensor_reduce", [mparts[:, ks:ks + 1],
                                                          sc_sb[:, epar, ks * 512:(ks + 1) * 512],
                                                          AX.X, ALU.max], waits=[c2])
                            c_pm.append(c3)
                        c_mx = kb.vop("tensor_reduce", [m_run, mparts[:, 0:n_ks], AX.X, ALU.max],
                                      waits=c_pm)
                        c_bm = kb.vop("tensor_scalar", [bm, m_run, -exp_scale, None],
                                      kwargs={"op0": ALU.mult}, waits=[c_mx])
                        c_ex = []
                        for ks in range(n_ks):
                            c2 = kb.act(exp_sb[:, epar, ks * 512:(ks + 1) * 512],
                                        sc_sb[:, epar, ks * 512:(ks + 1) * 512], AF.Exp,
                                        waits=[c_bm, rget(("exps", epar))],
                                        scale=exp_scale, bias=bm,
                                        accum=dparts[:, ks:ks + 1])
                            c_ex.append(c2)
                        rel[("scs", epar)] = c_ex[-1]
                        c_dn = kb.vop("tensor_reduce", [den, dparts[:, 0:n_ks], AX.X, ALU.add],
                                      waits=c_ex)
                        c_rc = kb.vop("reciprocal", [rcpd, den], waits=[c_dn])
                        c_nm = kb.vop("tensor_scalar", [exp_sb[:, epar, 0:n_ks * 512],
                                                        exp_sb[:, epar, 0:n_ks * 512], rcpd, None],
                                      kwargs={"op0": ALU.mult}, waits=[c_rc])
                        return (qt, epar, nkc, c_nm)

                    def av_stage(st):
                        qt, epar, nkc, c_nm = st
                        for half in range((nkc + 7) // 8):
                            j0 = half * 8
                            j1 = min(j0 + 8, nkc)
                            ins = [exp_sb[:, epar, j * P:(j + 1) * P] for j in range(j0, j1)]
                            outs = [ptr[:, (j - j0) * P:(j - j0 + 1) * P] for j in range(j0, j1)]
                            c_tr = kb.transpose_pack(outs, ins, ident_bf[:, :],
                                                     waits=[c_nm, rget("ptr")])
                            c_ev = kb.vop("tensor_copy",
                                          [attn_T[:, j0:j1, :], ptr[:, 0:(j1 - j0) * P]],
                                          waits=[c_tr])
                            rel["ptr"] = c_ev
                        rel[("exps", epar)] = c_tr
                        for kc in range(nkc):
                            w0 = []
                            if kc == 0:
                                w0 = [c_ev, rget("pAV")]
                            c_av = kb.mm1(pAV[:, 0:P], v_tm[:, kc, gl * P:(gl + 1) * P],
                                          attn_T[:, kc, :],
                                          start=(kc == 0), stop=(kc == nkc - 1),
                                          waits=w0, mark=(kc == nkc - 1))
                        c_o = kb.vop("tensor_copy", [ohb[:, qt * P:(qt + 1) * P], pAV[:, 0:P]],
                                     waits=[c_av, rget("ohb")])
                        rel["pAV"] = c_o

                    for qt in range(NTG):
                        st = softmax_stage(qt)
                        if pend is not None:
                            av_stage(pend)
                        pend = st
                    av_stage(pend)
                    rel[("qfree", par)] = ("mT", kb.cnt["mT"])
                    if maybe_stop(kind, f"oh{hl}", ohb[:, :], SG, waits=[("mV", kb.cnt["mV"])]):
                        return True
                    kb.wait("g", "mV", kb.cnt["mV"])
                    hdG = kb.sems["dG"]
                    v_oh = kb.inc("dG", 16)

                    def wr_oh(E, par_, hl=hl, hdG=hdG, shoh=shoh):
                        hg = par_ * 8 + hl
                        E.dma_start(out=shoh[hg, :, :], in_=ohb[:, :]).then_inc(hdG, 16)
                    kb.gswitch(wr_oh)
                    rel["ohb"] = ("dG", v_oh)
                kb.wait("g", "dG", kb.cnt["dG"])
                v_fl = kb.inc("dG", 16)
                hdG = kb.sems["dG"]

                def wr_fl(E, par_, hdG=hdG, ofl=ofl):
                    E.dma_start(out=sh_flag[ofl:ofl + 1, par_:par_ + 1],
                                in_=fvals_sb[:, ofl:ofl + 1]).then_inc(hdG, 16)
                kb.gswitch(wr_fl)
                kb.hw["g"]["dG"] = max(kb.hw["g"].get("dG", 0), kb.cnt["dG"])
                kb.gpoll(sh_flag, fvals_sb, ofl, f"o{kind}")
                if maybe_stop(kind, "ohx", None):
                    return True
            pes2 = ExitStack()
            with pes2:
                asm = pes2.enter_context(nc.sbuf_tensor(f"asm_{kind}", [P, 16, TOWN], BF16))
                wbufP = [pes2.enter_context(nc.sbuf_tensor(f"wbufP_{kind}", [P, NDC, 512], BF16))] * 2
                pP = [pes2.enter_context(nc.psum_tensor(f"pP_{kind}{i}", [P, 512], F32)) for i in range(2)]
                if kind == "b":
                    pout = pes2.enter_context(nc.sbuf_tensor("pout", [P, NT, D], F32))
                    go_rep = pes2.enter_context(nc.sbuf_tensor("go_rep", [P, D], BF16))
                    pj_t = pes2.enter_context(nc.sbuf_tensor("pj_t", [P, D], F32))
                    junk2 = pes2.enter_context(nc.sbuf_tensor("junk2", [P, D], BF16))
                    c_go = kb.dma("g", go_rep[:, :], ap["go"][None, :].broadcast_to((P, D)), stream="bc")
                kb.wait("g", "rX0", kb.cnt["rX0"])
                hrx = kb.sems["rX0"]
                v_as = kb.inc("rX0", 16)

                def rd_asm(E, par_, hrx=hrx, shoh=shoh):
                    E.dma_start(out=asm[:, :, :],
                                in_=shoh[:, :, par_ * TOWN:(par_ + 1) * TOWN]
                                .rearrange("h p q -> p h q")).then_inc(hrx, 16)
                kb.gswitch(rd_asm)
                c_asm = ("rX0", v_as)
                if maybe_stop(kind, "asm", asm[:, :, :], 16 * TOWN, waits=[c_asm]):
                    return True
                if DBG_STOP == (kind, "shoh"):
                    dbg_dump(shoh.rearrange("h p q -> p h q"), 16 * SG)
                    return True
                wsrc2 = ap["projT"] if kind == "a" else ap["wout"]
                rel2 = {}

                def rget2(key):
                    return rel2.get(key, NOW)

                for ob in range(4):
                    c_wp = kb.dma("g", wbufP[0][:, :, :],
                                  wsrc2[:, ob * 512:(ob + 1) * 512].rearrange("(c p) o -> p c o", p=P),
                                  waits=[rget2("wp")], stream="wP")
                    for tt in range(NT):
                        bank = pP[tt % 2]
                        pairs = [(asm[:, ec, tt * P:(tt + 1) * P], wbufP[0][:, ec, :])
                                 for ec in range(NDC)]
                        c_mm = kb.mmg(bank[:, :], pairs, waits=[c_wp, c_asm, rget2(("pP", tt % 2))])
                        if kind == "a":
                            c_add = kb.vop("tensor_tensor",
                                           [R[:, tt, ob * 512:(ob + 1) * 512],
                                            bank[:, :], R[:, tt, ob * 512:(ob + 1) * 512]],
                                           kwargs={"op": ALU.add}, waits=[c_mm])
                        else:
                            c_add = kb.vop("tensor_copy",
                                           [pout[:, tt, ob * 512:(ob + 1) * 512], bank[:, :]],
                                           waits=[c_mm])
                        rel2[("pP", tt % 2)] = c_add
                        rel2[("pt", tt)] = c_add
                    rel2["wp"] = ("mT", kb.cnt["mT"])
                    if maybe_stop(kind, f"p{ob}", R[:, :, :], NT * D,
                                  waits=[("mV", kb.cnt["mV"]), ("mS", kb.cnt["mS"]),
                                         ("mT", kb.cnt["mT"])]):
                        return True
                if kind == "b":
                    kb.wait("v", c_go[0], c_go[1])
                    for tt in range(NT):
                        rstd, murstd, c_st = ln_stats(pout[:, tt, :], junk2[:, :], col=tt % 2,
                                                      src_ready=[rget2(("pt", tt))])
                        cV = kb.vop("tensor_scalar", [pj_t[:, :], pout[:, tt, :], rstd, murstd],
                                    kwargs={"op0": ALU.mult, "op1": ALU.subtract},
                                    waits=[c_st, rget2("pj")])
                        cV2 = kb.vop("tensor_tensor", [pj_t[:, :], pj_t[:, :], go_rep[:, :]],
                                     kwargs={"op": ALU.mult}, waits=[cV])
                        cV3 = kb.vop("tensor_tensor", [R[:, tt, :], R[:, tt, :], pj_t[:, :]],
                                     kwargs={"op": ALU.add}, waits=[cV2])
                        rel2["pj"] = cV3
            kb.barrier()
            if maybe_stop(kind, "bar", R[:, :, :], NT * D, waits=[("mV", kb.cnt["mV"])]):
                return True

        def ffn_phase(tag):
            pes = ExitStack()
            with pes:
                xnf = pes.enter_context(nc.sbuf_tensor(f"xnf_{tag}", [P, NDC, TOWN], BF16))
                xg = [pes.enter_context(nc.sbuf_tensor(f"xg_{tag}{i}", [P, D], BF16)) for i in range(2)]
                junk = pes.enter_context(nc.sbuf_tensor(f"junk_{tag}", [P, D], BF16))
                g_blk = pes.enter_context(nc.sbuf_tensor(f"gblk_{tag}", [P, 2, TOWN], BF16))
                h_full = pes.enter_context(nc.sbuf_tensor(f"h_{tag}", [P, SNIC, TOWN], BF16))
                w13b = [pes.enter_context(nc.sbuf_tensor(f"w13_{tag}{i}", [P, NDC, 256], BF16))
                        for i in range(2)]
                w2b = [pes.enter_context(nc.sbuf_tensor(f"w2_{tag}{i}", [P, SNIC, 256], BF16))
                       for i in range(2)]
                pgu = [pes.enter_context(nc.psum_tensor(f"pgu_{tag}{i}", [P, 512], F32)) for i in range(4)]
                py = [pes.enter_context(nc.psum_tensor(f"py_{tag}{i}", [P, 512], F32)) for i in range(3)]
                ptr = pes.enter_context(nc.psum_tensor(f"ptr_{tag}", [P, 1024], BF16))

                rel = {}

                def rget(key):
                    return rel.get(key, NOW)

                for tt in range(NT):
                    par = tt % 2
                    c_xn = ln_to_xn(R[:, tt, :], xg[par][:, :], junk[:, :], col=par,
                                    out_free=[rget(("xg", par))])
                    for half in (0, 1):
                        ins = [xg[par][:, (8 * half + j) * P:(8 * half + j + 1) * P] for j in range(8)]
                        outs = [ptr[:, j * P:(j + 1) * P] for j in range(8)]
                        c_tr = kb.transpose_pack(outs, ins, ident_bf[:, :],
                                                 waits=[c_xn, rget("ptr")])
                        c_ev = kb.vop("tensor_copy",
                                      [xnf[:, 8 * half:8 * half + 8, tt * P:(tt + 1) * P], ptr[:, :]],
                                      waits=[c_tr])
                        rel["ptr"] = c_ev
                    rel[("xg", par)] = c_tr
                c_xnf = c_ev

                probs = gate_t[:, 0:16]
                top8 = gate_t[:, 16:24]
                mask = gate_t[:, 32:48]
                pm = gate_t[:, 48:64]
                nmax = sml[:, 18:19]
                tsum = sml[:, 19:20]
                rcp = sml[:, 20:21]
                for tt in range(NT):
                    bank = py[0]
                    pairs = [(xnf[:, dc, tt * P:(tt + 1) * P], gwT_sb[:, dc, :]) for dc in range(NDC)]
                    c_mm = kb.mmg(bank[:, 0:16], pairs, waits=[c_xnf, rget("pg")])
                    c_nm = kb.vop("tensor_reduce", [nmax, bank[:, 0:16], AX.X, ALU.max],
                                  kwargs={"negate": True}, waits=[c_mm, rget("gate")])
                    c_pr = kb.act(probs, bank[:, 0:16], AF.Exp, bias=nmax, waits=[c_nm])
                    rel["pg"] = c_pr
                    c_t8 = kb.vop("max", [top8, probs], waits=[c_pr])
                    c_mk = kb.vop("tensor_scalar", [mask, probs, gate_t[:, 21:22], None],
                                  kwargs={"op0": ALU.is_ge}, waits=[c_t8])
                    c_pm = kb.vop("scalar_tensor_tensor", [pm, probs, 1.0, mask],
                                  kwargs={"op0": ALU.mult, "op1": ALU.mult, "accum_out": tsum},
                                  waits=[c_mk])
                    c_rc = kb.vop("reciprocal", [rcp, tsum], waits=[c_pm])
                    c_c = kb.vop("tensor_scalar", [c_sb[:, tt, :], pm, rcp, None],
                                 kwargs={"op0": ALU.mult}, waits=[c_rc])
                    rel["gate"] = c_c
                c_gate = c_c
                rel[("py", 0)] = rel["pg"]

                def expert_cfg(e):
                    shared = e == NE
                    return {
                        "e": e,
                        "shared": shared,
                        "nic": SNIC if shared else NIC,
                        "fi": SFI if shared else FI,
                        "hpar": 0 if shared else (e % 2),
                        "w13src": ap["sw13T"] if shared else ap["w13T"][e],
                        "w2src": ap["sw2T"] if shared else ap["w2T"][e],
                    }

                gu_rot = [0]

                def emit_gu(cfg):
                    e, shared, nic, fi, hpar = (cfg["e"], cfg["shared"], cfg["nic"],
                                                cfg["fi"], cfg["hpar"])
                    hbase = 0 if shared else NIC * hpar
                    if shared:
                        kb.waits("t", [rget(("h", 0)), rget(("h", 1))])
                    nblk = (fi + 255) // 256
                    for j in range(nblk):
                        w = min(256, fi - 256 * j)
                        for is_g in (True, False):
                            c0 = 256 * j + (0 if is_g else fi)
                            par = 0 if is_g else 1
                            c_wd = kb.dma("g", w13b[par][:, :, 0:w],
                                          cfg["w13src"][:, c0:c0 + w].rearrange("(c p) o -> p c o", p=P),
                                          waits=[rget(("w13", par))], stream=f"w13{par}")
                            for oc in range(w // 128):
                                ic = 2 * j + oc
                                for th in (0, 1):
                                    bank = pgu[gu_rot[0] % 4]
                                    gu_rot[0] += 1
                                    pairs = [(w13b[par][:, dc, oc * P:(oc + 1) * P],
                                              xnf[:, dc, th * 512:(th + 1) * 512])
                                             for dc in range(NDC)]
                                    c_mm = kb.mmg(bank[:, :], pairs,
                                                  waits=[c_wd, c_xnf, rget(("pgu", (gu_rot[0] - 1) % 4))])
                                    if is_g:
                                        c_ev = kb.act(g_blk[:, oc, th * 512:(th + 1) * 512], bank[:, :],
                                                      AF.Silu, waits=[c_mm, rget("gblk")])
                                    else:
                                        c_ev = kb.vop("tensor_tensor",
                                                      [h_full[:, hbase + ic, th * 512:(th + 1) * 512],
                                                       bank[:, :], g_blk[:, oc, th * 512:(th + 1) * 512]],
                                                      kwargs={"op": ALU.mult},
                                                      waits=[c_mm, rget(("hfree", hpar))])
                                    rel[("pgu", (gu_rot[0] - 1) % 4)] = c_ev
                            rel[("w13", par)] = ("mT", kb.cnt["mT"])
                            if not is_g:
                                rel["gblk"] = c_ev
                    cfg["c_h"] = ("mV", kb.cnt["mV"])
                    cfg["c_h_s"] = ("mS", kb.cnt["mS"])

                def emit_y(cfg):
                    e, shared, nic, hpar = cfg["e"], cfg["shared"], cfg["nic"], cfg["hpar"]
                    hbase = 0 if shared else NIC * hpar
                    for ob in range(8):
                        par = ob % 2
                        c_w2 = kb.dma("g", w2b[par][:, 0:nic, :],
                                      cfg["w2src"][:, ob * 256:(ob + 1) * 256]
                                      .rearrange("(c p) o -> p c o", p=P),
                                      waits=[rget(("w2", par))], stream=f"w2{par}")
                        for tt in range(NT):
                            bank = py[(ob * NT + tt) % 3]
                            pairs = [(h_full[:, hbase + ic, tt * P:(tt + 1) * P], w2b[par][:, ic, :])
                                     for ic in range(nic)]
                            c_mm = kb.mmg(bank[:, 0:256], pairs,
                                          waits=[c_w2, cfg["c_h"], cfg["c_h_s"], c_gate,
                                                 rget(("py", (ob * NT + tt) % 3))])
                            if shared:
                                c_y = kb.vop("tensor_tensor",
                                             [R[:, tt, ob * 256:(ob + 1) * 256], bank[:, 0:256],
                                              R[:, tt, ob * 256:(ob + 1) * 256]],
                                             kwargs={"op": ALU.add}, waits=[c_mm])
                            else:
                                c_y = kb.vop("scalar_tensor_tensor",
                                             [R[:, tt, ob * 256:(ob + 1) * 256], bank[:, 0:256],
                                              c_sb[:, tt, e:e + 1], R[:, tt, ob * 256:(ob + 1) * 256]],
                                             kwargs={"op0": ALU.mult, "op1": ALU.add}, waits=[c_mm])
                            rel[("py", (ob * NT + tt) % 3)] = c_y
                        rel[("w2", par)] = ("mT", kb.cnt["mT"])
                    rel[("h", hpar)] = ("mT", kb.cnt["mT"])
                    if shared:
                        rel[("h", 0)] = rel[("h", 1)] = ("mT", kb.cnt["mT"])
                    rel[("hfree", hpar)] = rel[("h", hpar)]

                cfgs = [expert_cfg(e) for e in range(NE + 1)]
                for e in range(NE + 1):
                    emit_gu(cfgs[e])
                    emit_y(cfgs[e])
            kb.barrier()

        stopped = False
        if "a1" in phases:
            stopped = attn_phase("a")
        if not stopped and "f1" in phases:
            stopped = ffn_phase("f1")
        if not stopped and "a2" in phases:
            stopped = attn_phase("b")
        if not stopped and "f2" in phases:
            stopped = ffn_phase("f2")

        c_out = kb.dma("y", out_ap.rearrange("(a p) d -> p a d", p=P), R[:, :, :])
        kb.wait("y", "dY", c_out[1])

        with nc.Block() as blk:
            @blk.tensor
            def _(E):
                for f in kb.q["t"]:
                    f(E)

            @blk.vector
            def _(E):
                for f in kb.q["v"]:
                    f(E)

            @blk.scalar
            def _(E):
                for f in kb.q["s"]:
                    f(E)

            @blk.sync
            def _(E):
                for f in kb.q["y"]:
                    f(E)

            @blk.gpsimd
            def _(E):
                for f in kb.q["g"]:
                    f(E)
    return nc


_W_CACHE = {}


def _prep_shared(inputs, need_f=True):
    key = (id(inputs.get("L0_ffn_w13")), need_f)
    if _W_CACHE.get("key") == key:
        return _W_CACHE
    c = _W_CACHE
    c.clear()
    c["key"] = key
    c["projT"] = np.ascontiguousarray(inputs["L0_attn_a_weight_proj"].T.astype(np.float32))
    c["wkv"] = np.ascontiguousarray(inputs["L0_attn_b_Wkv"].astype(np.float32))
    c["wout"] = np.ascontiguousarray(inputs["L0_attn_b_Wout"].astype(np.float32))
    c["gn"] = np.ascontiguousarray(inputs["L0_attn_b_g_norm"].astype(np.float32))
    c["go"] = np.ascontiguousarray(inputs["L0_attn_b_g_out"].astype(np.float32))
    c["gwT"] = np.ascontiguousarray(inputs["L0_ffn_gate_w"].T.astype(np.float32))
    if need_f:
        c["w13T"] = np.ascontiguousarray(np.transpose(inputs["L0_ffn_w13"], (0, 2, 1)).astype(np.float32))
        c["w2T"] = np.ascontiguousarray(np.transpose(inputs["L0_ffn_w2"], (0, 2, 1)).astype(np.float32))
        c["sw13T"] = np.ascontiguousarray(inputs["L0_ffn_sw13"].T.astype(np.float32))
        c["sw2T"] = np.ascontiguousarray(inputs["L0_ffn_sw2"].T.astype(np.float32))
    c["ident"] = np.eye(P, dtype=np.float32)
    c["ones"] = np.ones((P, P), dtype=np.float32)
    masks = np.zeros((4, P, 512), dtype=np.float32)
    ql = np.arange(P)[:, None]
    kl = np.arange(512)[None, :]
    for j in range(4):
        masks[j] = np.where(kl <= ql + 128 * j, 0.0, -30000.0).astype(np.float32)
    c["masks"] = masks
    c["bvals"] = np.array([[1e-5, 1e-12, -4.0, -16.0, math.log(4.0), math.log(64.0), 0.0, 0.0]],
                          dtype=np.float32)
    return c


def _in_maps(inputs, phases):
    need_f = ("f1" in phases) or ("f2" in phases)
    c = _prep_shared(inputs, need_f)
    x = np.ascontiguousarray(inputs["L0_attn_a_x"].astype(np.float32))
    wqkv = inputs["L0_attn_a_weight_qkv"].astype(np.float32)
    wq_all = inputs["L0_attn_b_Wq"].astype(np.float32)
    nonce = np.random.randint(1, 2 ** 31 - 10, dtype=np.uint32)
    fvals = (nonce + 1 + np.arange(8, dtype=np.uint32)).reshape(1, 8).astype(np.uint32)
    maps = []
    for core in range(8):
        b, h = divmod(core, 2)
        rows = []
        for i in range(8):
            hg = h * 8 + i
            g, j = divmod(hg, 4)
            rows.append(wqkv[g * 768 + j * 128: g * 768 + (j + 1) * 128])
        for g in (2 * h, 2 * h + 1):
            rows.append(wqkv[g * 768 + 512: g * 768 + 640])
        for g in (2 * h, 2 * h + 1):
            rows.append(wqkv[g * 768 + 640: g * 768 + 768])
        qkvT = np.ascontiguousarray(np.concatenate(rows, axis=0).T)
        wq = np.ascontiguousarray(wq_all[:, h * 1024:(h + 1) * 1024])
        m = {
            "x_own": np.ascontiguousarray(x[b, h * TOWN:(h + 1) * TOWN]),
            "x_full": np.ascontiguousarray(x[b]),
            "qkvT": qkvT,
            "projT": c["projT"],
            "wq": wq,
            "wkv": c["wkv"],
            "wout": c["wout"],
            "gn": c["gn"],
            "go": c["go"],
            "gwT": c["gwT"],
            **({"w13T": c["w13T"], "w2T": c["w2T"],
                "sw13T": c["sw13T"], "sw2T": c["sw2T"]} if need_f else {}),
            "ident": c["ident"],
            "ones": c["ones"],
            "masks": c["masks"],
            "fvals": fvals,
            "bvals": c["bvals"],
        }
        maps.append(m)
    return maps


_NC_CACHE = {}


def kernel(_phases=("a1", "f1", "a2", "f2"), **inputs):
    maps = _in_maps(inputs, _phases)
    key = (tuple(_phases), DBG_STOP)
    if key not in _NC_CACHE:
        _NC_CACHE[key] = build_program(_phases)
    nc = _NC_CACHE[key]
    res = run_bass_kernel_spmd(nc, maps, core_ids=list(range(8)))
    out = np.empty((4, SG, D), dtype=np.float32)
    for core in range(8):
        b, h = divmod(core, 2)
        out[b, h * TOWN:(h + 1) * TOWN] = res.results[core]["out"]
    if DBG_STOP is not None:
        kernel.dbg = [res.results[c]["dbg"] for c in range(8)]
    return out


# version 05
# speedup vs baseline: 1.1957x; 1.1957x over previous
import math
from contextlib import ExitStack

import numpy as np
import concourse.bass as bass
import concourse.mybir as mybir
from concourse.bass_utils import run_bass_kernel_spmd

F32 = mybir.dt.float32
BF16 = mybir.dt.bfloat16
U32 = mybir.dt.uint32
AF = mybir.ActivationFunctionType
ALU = mybir.AluOpType
AX = mybir.AxisListType

P = 128
D = 2048
NDC = D // P
TOWN = 1024
NT = TOWN // P
SG = 2048
NTG = SG // P
NE = 16
FI = 1408
NIC = FI // P
SFI = 2816
SNIC = SFI // P
INV_SQRT128 = 1.0 / math.sqrt(128.0)

ENGS = ("t", "v", "s", "y", "g")
NOW = ("mT", 0)
DBG_STOP = None
BARRIER_ENGS = ("t", "v", "s", "y", "g")


class KB:
    def __init__(self, nc):
        self.nc = nc
        self.q = {e: [] for e in ENGS}
        self.sems = {}
        self.cnt = {}
        self.hw = {e: {} for e in ENGS}

    def sem_init(self, es, names):
        for n in names:
            self.sems[n] = es.enter_context(self.nc.semaphore(n))
            self.cnt[n] = 0

    def emit(self, e, fn):
        self.q[e].append(fn)

    def wait(self, e, sn, val):
        if val <= self.hw[e].get(sn, 0):
            return
        self.hw[e][sn] = val
        h = self.sems[sn]
        self.q[e].append(lambda E, h=h, v=val: E.wait_ge(h, v))

    def waits(self, e, lst):
        for w in lst:
            if w:
                self.wait(e, w[0], w[1])

    def inc(self, sn, n=1):
        self.cnt[sn] += n
        return self.cnt[sn]

    def dma(self, e, out_ap, in_ap, waits=(), stream=None, **kw):
        sn = stream if stream else ("dG" if e == "g" else "dY")
        self.waits(e, waits)
        if stream:
            self.wait(e, sn, self.cnt[sn])
        h = self.sems[sn]
        v = self.inc(sn, 16)
        self.emit(e, lambda E, o=out_ap, i=in_ap, h=h, kw=dict(kw):
                  E.dma_start(out=o, in_=i, **kw).then_inc(h, 16))
        return (sn, v)

    def mm1(self, psum_ap, lhsT, rhs, start, stop, waits=(), mark=False):
        self.waits("t", waits)
        if mark:
            h, v = self.sems["mT"], self.inc("mT")
            self.emit("t", lambda E, ps=psum_ap, l=lhsT, r=rhs, st=start, sp=stop, h=h:
                      E.matmul(ps, l, r, start=st, stop=sp).then_inc(h, 1))
            return ("mT", v)
        self.emit("t", lambda E, ps=psum_ap, l=lhsT, r=rhs, st=start, sp=stop:
                  E.matmul(ps, l, r, start=st, stop=sp))
        return None

    def mmg(self, psum_ap, pairs, waits=()):
        self.waits("t", waits)
        h, v = self.sems["mT"], self.inc("mT")
        n = len(pairs)

        def f(E, ps=psum_ap, pairs=list(pairs), h=h, n=n):
            for i, (l, r) in enumerate(pairs):
                mm = E.matmul(ps, l, r, start=(i == 0), stop=(i == n - 1))
            mm.then_inc(h, 1)
        self.emit("t", f)
        return ("mT", v)

    def transpose_pack(self, out_list, in_list, ident, waits=()):
        self.waits("t", waits)
        h, v = self.sems["mT"], self.inc("mT")

        def f(E, outs=list(out_list), ins=list(in_list), ident=ident, h=h):
            for o, i in zip(outs, ins):
                tr = E.transpose(o, i, ident)
            tr.then_inc(h, 1)
        self.emit("t", f)
        return ("mT", v)

    def act(self, out_ap, in_ap, func, waits=(), bias=None, scale=None, accum=None):
        self.waits("s", waits)
        h, v = self.sems["mS"], self.inc("mS")

        def f(E, o=out_ap, i=in_ap, func=func, bias=bias, scale=scale, accum=accum, h=h):
            kw = {}
            if bias is not None:
                kw["bias"] = bias
            if scale is not None:
                kw["scale"] = scale
            if accum is not None:
                kw["accum_out"] = accum
            E.activation(o, i, func, **kw).then_inc(h, 1)
        self.emit("s", f)
        return ("mS", v)

    def scopy(self, out_ap, in_ap, waits=()):
        self.waits("s", waits)
        h, v = self.sems["mS"], self.inc("mS")
        self.emit("s", lambda E, o=out_ap, i=in_ap, h=h: E.copy(o, i).then_inc(h, 1))
        return ("mS", v)

    def vop(self, name, args, waits=(), kwargs=None):
        self.waits("v", waits)
        h, v = self.sems["mV"], self.inc("mV")
        kwargs = dict(kwargs or {})

        def f(E, name=name, args=tuple(args), kwargs=kwargs, h=h):
            getattr(E, name)(*args, **kwargs).then_inc(h, 1)
        self.emit("v", f)
        return ("mV", v)

    def gswitch(self, body, waits=()):
        self.waits("g", waits)
        nc = self.nc

        def f(E, body=body):
            for arm in nc.switch_core_id([E], n=128):
                b = arm.value if hasattr(arm, "value") else int(arm)
                body(E, (b % 8) % 2)
        self.emit("g", f)

    def gpoll(self, flag_ap, fvals_sb, k, tag):
        def f(E, k=k, tag=tag):
            with E.register(f"pa{tag}") as ra, E.register(f"pn{tag}") as rn:
                E.load(rn, fvals_sb[0:1, k:k + 1])
                for slot in (0, 1):
                    E.reg_mov(ra, 1)
                    with E.While(ra):
                        E.load(ra, flag_ap[k:k + 1, slot:slot + 1])
                        E.reg_alu(ra, ra, rn, ALU.not_equal)
        self.emit("g", f)

    def barrier(self):
        marks = [("t", "mT", self.cnt["mT"]), ("v", "mV", self.cnt["mV"]),
                 ("s", "mS", self.cnt["mS"]), ("y", "dY", self.cnt["dY"]),
                 ("g", "dG", self.cnt["dG"])]
        for e in ENGS:
            for e2, sn, v in marks:
                if e2 != e:
                    self.wait(e, sn, v)
        for sn in ("bc", "xl", "wA", "wP", "w130", "w131", "w20", "w21", "rX0", "rX1"):
            for e in ENGS:
                self.wait(e, sn, self.cnt[sn])


def build_program(phases):
    nc = bass.Bass("TRN2", target_bir_lowering=False)
    ap = {}

    def din(name, shape, dtype=F32):
        ap[name] = nc.dram_tensor(name, shape, dtype, kind="ExternalInput").ap()

    din("x_own", [TOWN, D])
    din("x_full", [SG, D])
    din("qkvT", [D, 1536])
    din("projT", [D, D])
    din("wq", [D, 1024])
    din("wkv", [D, 256])
    din("wout", [D, D])
    din("gn", [D])
    din("go", [D])
    need_f = ("f1" in phases) or ("f2" in phases)
    din("gwT", [D, NE])
    if need_f:
        din("w13T", [NE, D, SFI])
        din("w2T", [NE, FI, D])
        din("sw13T", [D, 2 * SFI])
        din("sw2T", [SFI, D])
    din("ident", [P, P])
    din("ones", [P, P])
    din("masks", [4, P, 512])
    din("fvals", [1, 8], U32)
    din("bvals", [1, 8])
    out_ap = nc.dram_tensor("out", [TOWN, D], F32, kind="ExternalOutput").ap()
    dbg_ap = nc.dram_tensor("dbg", [P, 32768], F32, kind="ExternalOutput").ap()

    sh_xn = nc.dram_tensor("sh_xn", [SG, D], BF16, addr_space="Shared").ap()
    sh_oh = {"a": nc.dram_tensor("sh_oha", [16, P, SG], BF16, addr_space="Shared").ap(),
             "b": nc.dram_tensor("sh_ohb", [16, P, SG], BF16, addr_space="Shared").ap()}
    sh_flag = nc.dram_tensor("sh_flag", [8, 2], U32, addr_space="Shared").ap()

    es = ExitStack()
    with es:
        kb = KB(nc)
        kb.sem_init(es, ["dG", "dY", "mT", "mV", "mS", "bc", "xl", "wA", "wP",
                         "w130", "w131", "w20", "w21", "rX0", "rX1"])

        R = es.enter_context(nc.sbuf_tensor("R", [P, NT, D], F32))
        ident_bf = es.enter_context(nc.sbuf_tensor("ident_bf", [P, P], BF16))
        ones_bf = es.enter_context(nc.sbuf_tensor("ones_bf", [P, P], BF16))
        masks_bf = es.enter_context(nc.sbuf_tensor("masks_bf", [P, 4, 512], BF16))
        gwT_sb = es.enter_context(nc.sbuf_tensor("gwT_sb", [P, NDC, NE], BF16))
        c_sb = es.enter_context(nc.sbuf_tensor("c_sb", [P, NT, NE], F32))
        fvals_sb = es.enter_context(nc.sbuf_tensor("fvals_sb", [1, 8], U32))
        sml = es.enter_context(nc.sbuf_tensor("sml", [P, 48], F32))
        bv = es.enter_context(nc.sbuf_tensor("bv", [P, 8], F32))
        gate_t = es.enter_context(nc.sbuf_tensor("gate_t", [P, 64], F32))

        kb.dma("g", ident_bf[:, :], ap["ident"][:, :], stream="bc")
        kb.dma("g", ones_bf[:, :], ap["ones"][:, :], stream="bc")
        kb.dma("g", masks_bf[:, :, :], ap["masks"].rearrange("m p q -> p m q"), stream="bc")
        kb.dma("g", gwT_sb[:, :, :], ap["gwT"].rearrange("(c p) e -> p c e", p=P), stream="bc")
        kb.dma("g", bv[:, :], ap["bvals"][0, None, :].broadcast_to((P, 8)), stream="bc")
        c_consts = kb.dma("g", fvals_sb[:, :], ap["fvals"][:, :], stream="bc")
        c_rinit = kb.dma("y", R[:, :, :], ap["x_own"].rearrange("(a p) d -> p a d", p=P),
                         stream="xl")
        for e in ("t", "v", "s", "g"):
            kb.wait(e, c_consts[0], c_consts[1])
        for e in ("t", "v", "s"):
            kb.wait(e, c_rinit[0], c_rinit[1])

        def dbg_dump(src_ap, nelem, waits=()):
            kb.dma("g", dbg_ap[:, 0:nelem], src_ap, waits=list(waits))

        def maybe_stop(kind, stage, buf_ap=None, nelem=0, waits=()):
            if DBG_STOP == (kind, stage):
                if buf_ap is not None:
                    dbg_dump(buf_ap, nelem, waits)
                return True
            return False

        def ln_stats(src_ap, junk_ap, col, src_ready=()):
            msum = sml[:, 4 * col + 0:4 * col + 1]
            s_x = sml[:, 4 * col + 1:4 * col + 2]
            mu = sml[:, 4 * col + 2:4 * col + 3]
            rstd = sml[:, 4 * col + 3:4 * col + 4]
            cS = kb.act(junk_ap, src_ap, AF.Square, waits=list(src_ready), accum=msum)
            cV1 = kb.vop("tensor_reduce", [s_x, src_ap, AX.X, ALU.add], waits=list(src_ready))
            cV2 = kb.vop("tensor_scalar", [mu, s_x, 1.0 / D, None],
                         kwargs={"op0": ALU.mult}, waits=[cV1])
            cV3 = kb.vop("tensor_scalar", [msum, msum, 1.0 / D, None],
                         kwargs={"op0": ALU.mult}, waits=[cS])
            cV4 = kb.vop("scalar_tensor_tensor", [msum, mu, mu, msum],
                         kwargs={"op0": ALU.mult, "op1": ALU.subtract}, waits=[cV2, cV3])
            cS2 = kb.act(rstd, msum, AF.Ln, waits=[cV4], bias=bv[:, 0:1], scale=-1.0)
            cS3 = kb.act(rstd, rstd, AF.Exp, waits=[cS2], scale=-0.5)
            cV5 = kb.vop("tensor_tensor", [s_x, mu, rstd], kwargs={"op": ALU.mult}, waits=[cS3])
            return rstd, s_x, cV5

        def ln_to_xn(src_ap, out_ap, junk_ap, col, gamma_rep=None, tmp_f32=None,
                     src_ready=(), out_free=(), tmp_free=()):
            rstd, murstd, c_st = ln_stats(src_ap, junk_ap, col, src_ready)
            if gamma_rep is None:
                return kb.vop("tensor_scalar", [out_ap, src_ap, rstd, murstd],
                              kwargs={"op0": ALU.mult, "op1": ALU.subtract},
                              waits=[c_st] + list(out_free))
            cV = kb.vop("tensor_scalar", [tmp_f32, src_ap, rstd, murstd],
                        kwargs={"op0": ALU.mult, "op1": ALU.subtract},
                        waits=[c_st] + list(tmp_free))
            return kb.vop("tensor_tensor", [out_ap, tmp_f32, gamma_rep],
                          kwargs={"op": ALU.mult}, waits=[cV] + list(out_free))

        def attn_phase(kind):
            causal = kind == "a"
            xfl = 0 if kind == "a" else 1
            ofl = 2 if kind == "a" else 3
            shoh = sh_oh[kind]
            pes = ExitStack()
            with pes:
                xnf = pes.enter_context(nc.sbuf_tensor(f"xnf_{kind}", [P, NDC, SG], BF16))
                xg = [pes.enter_context(nc.sbuf_tensor(f"xg_{kind}{i}", [P, D], BF16)) for i in range(2)]
                wbufA = pes.enter_context(nc.sbuf_tensor(f"wbufA_{kind}", [P, NDC, 256], BF16))
                sc_sb = pes.enter_context(nc.sbuf_tensor(f"scs_{kind}", [P, 2, SG], F32))
                ngr = 2 if kind == "a" else 1
                k_fm = pes.enter_context(nc.sbuf_tensor(f"kfm_{kind}", [P, ngr, SG], BF16))
                v_tm = pes.enter_context(nc.sbuf_tensor(f"vtm_{kind}", [P, NTG, 128 * ngr], BF16))
                q_bf = [pes.enter_context(nc.sbuf_tensor(f"qbf_{kind}{i}", [P, SG], BF16)) for i in range(2)]
                ohb = pes.enter_context(nc.sbuf_tensor(f"oh_{kind}", [P, SG], BF16))
                exp_sb = pes.enter_context(nc.sbuf_tensor(f"exps_{kind}", [P, 2, SG], BF16))
                junk = exp_sb[:, 0, :].tensor if False else None
                attn_T = pes.enter_context(nc.sbuf_tensor(f"attnT_{kind}", [P, NTG, P], BF16))
                if kind == "b":
                    rtmp = [pes.enter_context(nc.sbuf_tensor(f"rtmp_{kind}", [P, 512], F32))]
                xtmpf = sc_sb[:, 0, :]
                if kind == "b":
                    gn_rep = pes.enter_context(nc.sbuf_tensor("gn_rep", [P, D], BF16))
                    ktmp = pes.enter_context(nc.sbuf_tensor("ktmp", [P, P], BF16))
                    q2t = pes.enter_context(nc.sbuf_tensor("q2t", [P, 512], BF16))
                pSc = [pes.enter_context(nc.psum_tensor(f"pSc_{kind}{i}", [P, 512], F32))
                       for i in range(2)]
                pq = [pes.enter_context(nc.psum_tensor(f"pq_{kind}{i}", [P, 512], F32))
                      for i in range(2)]
                pD = pes.enter_context(nc.psum_tensor(f"pD_{kind}", [P, 512], F32))
                pAV = pes.enter_context(nc.psum_tensor(f"pAV_{kind}", [P, 512], F32))
                ptr = pes.enter_context(nc.psum_tensor(f"ptr_{kind}", [P, 1024], BF16))

                rel = {}

                def rget(key):
                    return rel.get(key, NOW)

                if kind == "b":
                    c_gn = kb.dma("g", gn_rep[:, :], ap["gn"][None, :].broadcast_to((P, D)), stream="bc")
                    kb.wait("v", c_gn[0], c_gn[1])

                if kind == "a":
                    for gt in range(NTG):
                        par = gt % 2
                        c_ld = kb.dma("y", xtmpf, ap["x_full"][gt * P:(gt + 1) * P, :],
                                      waits=[rget("xtmpf")], stream="xl")
                        for e in ("v", "s"):
                            kb.wait(e, c_ld[0], c_ld[1])
                        c_xn = ln_to_xn(xtmpf, xg[par][:, :], exp_sb[:, 0, :], col=par,
                                        src_ready=[c_ld], out_free=[rget(("xg", par))])
                        rel["xtmpf"] = c_xn
                        for half in (0, 1):
                            ins = [xg[par][:, (8 * half + j) * P:(8 * half + j + 1) * P] for j in range(8)]
                            outs = [ptr[:, j * P:(j + 1) * P] for j in range(8)]
                            c_tr = kb.transpose_pack(outs, ins, ident_bf[:, :],
                                                     waits=[c_xn, rget("ptr")])
                            c_ev = kb.vop("tensor_copy",
                                          [xnf[:, 8 * half:8 * half + 8, gt * P:(gt + 1) * P], ptr[:, :]],
                                          waits=[c_tr])
                            rel["ptr"] = c_ev
                        rel[("xg", par)] = c_tr
                    c_xnf = c_ev
                else:
                    for tt in range(NT):
                        par = tt % 2
                        c_xn = ln_to_xn(R[:, tt, :], xg[par][:, :], exp_sb[:, 0, :], col=par,
                                        gamma_rep=gn_rep[:, :], tmp_f32=xtmpf,
                                        tmp_free=[rget("xtmpf")],
                                        out_free=[rget(("xg", par))])
                        rel["xtmpf"] = c_xn
                        kb.wait("g", "mV", c_xn[1])
                        hdG = kb.sems["dG"]
                        v1 = kb.inc("dG", 16)

                        def wr_xn(E, par_, tt=tt, par=par, hdG=hdG):
                            E.dma_start(
                                out=sh_xn[par_ * TOWN + tt * P:par_ * TOWN + (tt + 1) * P, :],
                                in_=xg[par][:, :]).then_inc(hdG, 16)
                        kb.gswitch(wr_xn)
                        rel[("xg", par)] = ("dG", v1)
                    kb.wait("g", "dG", kb.cnt["dG"])
                    v2 = kb.inc("dG", 16)
                    hdG = kb.sems["dG"]

                    def wr_xfl(E, par_, hdG=hdG, xfl=xfl):
                        E.dma_start(out=sh_flag[xfl:xfl + 1, par_:par_ + 1],
                                    in_=fvals_sb[:, xfl:xfl + 1]).then_inc(hdG, 16)
                    kb.gswitch(wr_xfl)
                    kb.hw["g"]["dG"] = max(kb.hw["g"].get("dG", 0), v2)
                    kb.gpoll(sh_flag, fvals_sb, xfl, f"x{kind}")
                    for gt in range(NTG):
                        par = gt % 2
                        c_ld = kb.dma("g", xg[par][:, :], sh_xn[gt * P:(gt + 1) * P, :],
                                      waits=[rget(("xg", par))], stream=f"rX{par}")
                        for half in (0, 1):
                            ins = [xg[par][:, (8 * half + j) * P:(8 * half + j + 1) * P] for j in range(8)]
                            outs = [ptr[:, j * P:(j + 1) * P] for j in range(8)]
                            c_tr = kb.transpose_pack(outs, ins, ident_bf[:, :],
                                                     waits=[c_ld, rget("ptr")])
                            c_ev = kb.vop("tensor_copy",
                                          [xnf[:, 8 * half:8 * half + 8, gt * P:(gt + 1) * P], ptr[:, :]],
                                          waits=[c_tr])
                            rel["ptr"] = c_ev
                        rel[("xg", par)] = c_tr
                    c_xnf = c_ev

                if maybe_stop(kind, "xnf", xnf[:, :, :], NDC * SG, waits=[c_xnf]):
                    return True
                if kind == "a":
                    c_w = kb.dma("g", wbufA[:, :, :],
                                 ap["qkvT"][:, 1024:1280].rearrange("(c p) o -> p c o", p=P),
                                 stream="wA")
                    for gl in range(2):
                        for qs in range(4):
                            pairs = [(wbufA[:, dc, gl * P:(gl + 1) * P],
                                      xnf[:, dc, qs * 512:(qs + 1) * 512]) for dc in range(NDC)]
                            c_mm = kb.mmg(pD[:, :], pairs, waits=[c_w, c_xnf, rget("pkv")])
                            c_k = kb.scopy(k_fm[:, gl, qs * 512:(qs + 1) * 512], pD[:, :], waits=[c_mm])
                            rel["pkv"] = c_k
                    c_w = kb.dma("g", wbufA[:, :, :],
                                 ap["qkvT"][:, 1280:1536].rearrange("(c p) o -> p c o", p=P),
                                 waits=[("mT", kb.cnt["mT"])], stream="wA")
                    for gt in range(NTG):
                        bank = pAV if gt % 2 == 0 else pq[0]
                        pairs = [(xnf[:, dc, gt * P:(gt + 1) * P], wbufA[:, dc, :])
                                 for dc in range(NDC)]
                        c_mm = kb.mmg(bank[:, 0:256], pairs, waits=[c_w, c_xnf, rget(("pS", gt % 2))])
                        c_v = kb.vop("tensor_copy", [v_tm[:, gt, :], bank[:, 0:256]], waits=[c_mm])
                        rel[("pS", gt % 2)] = c_v
                else:
                    c_w = kb.dma("g", wbufA[:, :, 0:256],
                                 ap["wkv"].rearrange("(c p) o -> p c o", p=P), stream="wA")
                    ln4 = math.log(4.0)
                    for gt in range(NTG):
                        pairs = [(xnf[:, dc, gt * P:(gt + 1) * P], wbufA[:, dc, 0:256])
                                 for dc in range(NDC)]
                        c_mm = kb.mmg(pD[:, 0:256], pairs, waits=[c_w, c_xnf, rget("pkv")])
                        ksum = sml[:, 16:17]
                        krs = sml[:, 17:18]
                        cq = kb.act(exp_sb[:, 0, 0:P], pD[:, 0:P], AF.Square, waits=[c_mm], accum=ksum)
                        cl = kb.act(krs, ksum, AF.Ln, waits=[cq], bias=bv[:, 1:2])
                        ce = kb.act(krs, krs, AF.Exp, waits=[cl], scale=-0.5, bias=bv[:, 4:5])
                        ck = kb.vop("tensor_scalar", [ktmp[:, :], pD[:, 0:P], krs, None],
                                    kwargs={"op0": ALU.mult}, waits=[ce, rget("ktmp")])
                        cv = kb.vop("tensor_copy", [v_tm[:, gt, 0:P], pD[:, P:256]], waits=[c_mm])
                        rel["pkv"] = cv
                        c_tr = kb.transpose_pack([ptr[:, 0:P]], [ktmp[:, :]], ident_bf[:, :],
                                                 waits=[ck, rget("ptr")])
                        c_kf = kb.vop("tensor_copy", [k_fm[:, 0, gt * P:(gt + 1) * P], ptr[:, 0:P]],
                                      waits=[c_tr])
                        rel["ptr"] = c_kf
                        rel["ktmp"] = c_tr
                kv_done = ("mT", kb.cnt["mT"])
                if maybe_stop(kind, "kv", k_fm[:, :, :], ngr * SG,
                              waits=[("mV", kb.cnt["mV"]), ("mS", kb.cnt["mS"])]):
                    return True
                if DBG_STOP == (kind, "vv"):
                    dbg_dump(v_tm[:, :, :], NTG * 128 * ngr, waits=[("mV", kb.cnt["mV"])])
                    return True

                exp_scale = INV_SQRT128 if causal else 1.0
                ln64 = math.log(64.0)
                wsrc = ap["qkvT"] if kind == "a" else ap["wq"]
                c_wq = None
                for hl in range(8):
                    par = hl % 2
                    gl = hl // 4 if kind == "a" else 0
                    if hl % 2 == 0:
                        blk = hl // 2
                        c_wq = kb.dma("g", wbufA[:, :, :],
                                      wsrc[:, blk * 256:(blk + 1) * 256].rearrange("(c p) o -> p c o", p=P),
                                      waits=[("mT", kb.cnt["mT"])], stream="wA")
                    col0 = (hl % 2) * P
                    for qs in range(4):
                        bank = pq[qs % 2]
                        pairs = [(wbufA[:, dc, col0:col0 + P], xnf[:, dc, qs * 512:(qs + 1) * 512])
                                 for dc in range(NDC)]
                        c_mm = kb.mmg(bank[:, :], pairs,
                                      waits=[c_wq, c_xnf, rget(("pq", qs % 2))])
                        if kind == "a":
                            c_q = kb.scopy(q_bf[par][:, qs * 512:(qs + 1) * 512], bank[:, :],
                                           waits=[c_mm, rget(("qfree", par))])
                            rel[("pq", qs % 2)] = c_q
                        else:
                            cq2 = kb.act(q2t[:, :], bank[:, :], AF.Square,
                                         waits=[c_mm, rget("q2t")])
                            c_den = kb.mmg(pD[:, :], [(ones_bf[:, :], q2t[:, :])],
                                           waits=[cq2, rget("pD")])
                            rel["q2t"] = c_den
                            c_l = kb.act(rtmp[0][:, :], pD[:, :], AF.Ln,
                                         waits=[c_den, rget(("rt", 0))], scale=256.0, bias=bv[:, 1:2])
                            c_e = kb.act(rtmp[0][:, :], rtmp[0][:, :], AF.Exp, waits=[c_l],
                                         scale=-0.5, bias=bv[:, 5:6])
                            rel["pD"] = c_e
                            c_q = kb.vop("tensor_tensor",
                                         [q_bf[par][:, qs * 512:(qs + 1) * 512], bank[:, :], rtmp[0][:, :]],
                                         kwargs={"op": ALU.mult},
                                         waits=[c_e, rget(("qfree", par))])
                            rel[("pq", qs % 2)] = c_q
                            rel[("rt", 0)] = c_q
                    q_done = c_q

                    def stage_A(qt):
                        epar = qt % 2
                        n_ks = (qt // 4 + 1) if causal else 4
                        mb = 24 + 12 * epar
                        m_run = sml[:, mb:mb + 1]
                        bm = sml[:, mb + 1:mb + 2]
                        mparts = sml[:, mb + 8:mb + 12]
                        c_pm = []
                        for ks in range(n_ks):
                            bank = pSc[ks % 2]
                            c1 = kb.mmg(bank[:, :],
                                        [(q_bf[par][:, qt * P:(qt + 1) * P],
                                          k_fm[:, gl, ks * 512:(ks + 1) * 512])],
                                        waits=[q_done, kv_done, rget(("psc", ks % 2))])
                            dst = sc_sb[:, epar, ks * 512:(ks + 1) * 512]
                            if ks % 2 == 0:
                                c2 = kb.scopy(dst, bank[:, :], waits=[c1, rget(("scs", epar))])
                            else:
                                c2 = kb.vop("tensor_copy", [dst, bank[:, :]],
                                            waits=[c1, rget(("scs", epar))])
                            rel[("psc", ks % 2)] = c2
                            diag = causal and ks == qt // 4
                            if diag:
                                c2 = kb.vop("tensor_tensor", [dst, dst, masks_bf[:, qt % 4, :]],
                                            kwargs={"op": ALU.add}, waits=[c2])
                            c3 = kb.vop("tensor_reduce", [mparts[:, ks:ks + 1], dst,
                                                          AX.X, ALU.max], waits=[c2])
                            c_pm.append(c3)
                        c_mx = kb.vop("tensor_reduce", [m_run, mparts[:, 0:n_ks], AX.X, ALU.max],
                                      waits=c_pm + [rget(("bm", epar))])
                        c_bm = kb.vop("tensor_scalar", [bm, m_run, -exp_scale, None],
                                      kwargs={"op0": ALU.mult}, waits=[c_mx])
                        rel[("sca", epar)] = c_bm
                        return c_bm

                    def stage_X(qt):
                        epar = qt % 2
                        n_ks = (qt // 4 + 1) if causal else 4
                        mb = 24 + 12 * epar
                        bm = sml[:, mb + 1:mb + 2]
                        dparts = sml[:, mb + 2:mb + 6]
                        c_ex = []
                        for ks in range(n_ks):
                            c2 = kb.act(exp_sb[:, epar, ks * 512:(ks + 1) * 512],
                                        sc_sb[:, epar, ks * 512:(ks + 1) * 512], AF.Exp,
                                        waits=[rget(("sca", epar)), rget(("exps", epar))],
                                        scale=exp_scale, bias=bm,
                                        accum=dparts[:, ks:ks + 1])
                            c_ex.append(c2)
                        rel[("scs", epar)] = c_ex[-1]
                        rel[("bm", epar)] = c_ex[-1]
                        return c_ex

                    def stage_L(qt, c_ex):
                        epar = qt % 2
                        n_ks = (qt // 4 + 1) if causal else 4
                        mb = 24 + 12 * epar
                        dparts = sml[:, mb + 2:mb + 6]
                        den = sml[:, mb + 6:mb + 7]
                        rcpd = sml[:, mb + 7:mb + 8]
                        c_dn = kb.vop("tensor_reduce", [den, dparts[:, 0:n_ks], AX.X, ALU.add],
                                      waits=c_ex)
                        c_rc = kb.vop("reciprocal", [rcpd, den], waits=[c_dn])
                        c_nm = kb.vop("tensor_scalar", [exp_sb[:, epar, 0:n_ks * 512],
                                                        exp_sb[:, epar, 0:n_ks * 512], rcpd, None],
                                      kwargs={"op0": ALU.mult}, waits=[c_rc])
                        return c_nm

                    def stage_T(qt, c_nm):
                        epar = qt % 2
                        nkc = (qt + 1) if causal else NTG
                        for half in range((nkc + 7) // 8):
                            j0 = half * 8
                            j1 = min(j0 + 8, nkc)
                            ins = [exp_sb[:, epar, j * P:(j + 1) * P] for j in range(j0, j1)]
                            outs = [ptr[:, (j - j0) * P:(j - j0 + 1) * P] for j in range(j0, j1)]
                            c_tr = kb.transpose_pack(outs, ins, ident_bf[:, :],
                                                     waits=[c_nm, rget("ptr")])
                            c_ev = kb.scopy(attn_T[:, j0:j1, :], ptr[:, 0:(j1 - j0) * P],
                                            waits=[c_tr])
                            rel["ptr"] = c_ev
                        rel[("exps", epar)] = c_tr
                        for kc in range(nkc):
                            w0 = []
                            if kc == 0:
                                w0 = [c_ev, rget("pAV")]
                            c_av = kb.mm1(pAV[:, 0:P], v_tm[:, kc, gl * P:(gl + 1) * P],
                                          attn_T[:, kc, :],
                                          start=(kc == 0), stop=(kc == nkc - 1),
                                          waits=w0, mark=(kc == nkc - 1))
                        c_o = kb.vop("tensor_copy", [ohb[:, qt * P:(qt + 1) * P], pAV[:, 0:P]],
                                     waits=[c_av, rget("ohb")])
                        rel["pAV"] = c_o

                    st = {}
                    for c in range(NTG + 2):
                        if c < NTG:
                            stage_A(c)
                        if 1 <= c <= NTG:
                            ex = stage_X(c - 1)
                            st[c - 1] = stage_L(c - 1, ex)
                        if c >= 2:
                            stage_T(c - 2, st.pop(c - 2))
                    rel[("qfree", par)] = ("mT", kb.cnt["mT"])
                    if maybe_stop(kind, f"oh{hl}", ohb[:, :], SG, waits=[("mV", kb.cnt["mV"])]):
                        return True
                    kb.wait("g", "mV", kb.cnt["mV"])
                    hdG = kb.sems["dG"]
                    v_oh = kb.inc("dG", 16)

                    def wr_oh(E, par_, hl=hl, hdG=hdG, shoh=shoh):
                        hg = par_ * 8 + hl
                        E.dma_start(out=shoh[hg, :, :], in_=ohb[:, :]).then_inc(hdG, 16)
                    kb.gswitch(wr_oh)
                    rel["ohb"] = ("dG", v_oh)
                kb.wait("g", "dG", kb.cnt["dG"])
                v_fl = kb.inc("dG", 16)
                hdG = kb.sems["dG"]

                def wr_fl(E, par_, hdG=hdG, ofl=ofl):
                    E.dma_start(out=sh_flag[ofl:ofl + 1, par_:par_ + 1],
                                in_=fvals_sb[:, ofl:ofl + 1]).then_inc(hdG, 16)
                kb.gswitch(wr_fl)
                kb.hw["g"]["dG"] = max(kb.hw["g"].get("dG", 0), kb.cnt["dG"])
                kb.gpoll(sh_flag, fvals_sb, ofl, f"o{kind}")
                if maybe_stop(kind, "ohx", None):
                    return True
            pes2 = ExitStack()
            with pes2:
                asm = pes2.enter_context(nc.sbuf_tensor(f"asm_{kind}", [P, 16, TOWN], BF16))
                wbufP = [pes2.enter_context(nc.sbuf_tensor(f"wbufP_{kind}", [P, NDC, 512], BF16))] * 2
                pP = [pes2.enter_context(nc.psum_tensor(f"pP_{kind}{i}", [P, 512], F32)) for i in range(2)]
                if kind == "b":
                    pout = pes2.enter_context(nc.sbuf_tensor("pout", [P, NT, D], F32))
                    go_rep = pes2.enter_context(nc.sbuf_tensor("go_rep", [P, D], BF16))
                    pj_t = pes2.enter_context(nc.sbuf_tensor("pj_t", [P, D], F32))
                    junk2 = pes2.enter_context(nc.sbuf_tensor("junk2", [P, D], BF16))
                    c_go = kb.dma("g", go_rep[:, :], ap["go"][None, :].broadcast_to((P, D)), stream="bc")
                kb.wait("g", "rX0", kb.cnt["rX0"])
                hrx = kb.sems["rX0"]
                v_as = kb.inc("rX0", 16)

                def rd_asm(E, par_, hrx=hrx, shoh=shoh):
                    E.dma_start(out=asm[:, :, :],
                                in_=shoh[:, :, par_ * TOWN:(par_ + 1) * TOWN]
                                .rearrange("h p q -> p h q")).then_inc(hrx, 16)
                kb.gswitch(rd_asm)
                c_asm = ("rX0", v_as)
                if maybe_stop(kind, "asm", asm[:, :, :], 16 * TOWN, waits=[c_asm]):
                    return True
                if DBG_STOP == (kind, "shoh"):
                    dbg_dump(shoh.rearrange("h p q -> p h q"), 16 * SG)
                    return True
                wsrc2 = ap["projT"] if kind == "a" else ap["wout"]
                rel2 = {}

                def rget2(key):
                    return rel2.get(key, NOW)

                for ob in range(4):
                    c_wp = kb.dma("g", wbufP[0][:, :, :],
                                  wsrc2[:, ob * 512:(ob + 1) * 512].rearrange("(c p) o -> p c o", p=P),
                                  waits=[rget2("wp")], stream="wP")
                    for tt in range(NT):
                        bank = pP[tt % 2]
                        pairs = [(asm[:, ec, tt * P:(tt + 1) * P], wbufP[0][:, ec, :])
                                 for ec in range(NDC)]
                        c_mm = kb.mmg(bank[:, :], pairs, waits=[c_wp, c_asm, rget2(("pP", tt % 2))])
                        if kind == "a":
                            c_add = kb.vop("tensor_tensor",
                                           [R[:, tt, ob * 512:(ob + 1) * 512],
                                            bank[:, :], R[:, tt, ob * 512:(ob + 1) * 512]],
                                           kwargs={"op": ALU.add}, waits=[c_mm])
                        else:
                            c_add = kb.vop("tensor_copy",
                                           [pout[:, tt, ob * 512:(ob + 1) * 512], bank[:, :]],
                                           waits=[c_mm])
                        rel2[("pP", tt % 2)] = c_add
                        rel2[("pt", tt)] = c_add
                    rel2["wp"] = ("mT", kb.cnt["mT"])
                    if maybe_stop(kind, f"p{ob}", R[:, :, :], NT * D,
                                  waits=[("mV", kb.cnt["mV"]), ("mS", kb.cnt["mS"]),
                                         ("mT", kb.cnt["mT"])]):
                        return True
                if kind == "b":
                    kb.wait("v", c_go[0], c_go[1])
                    for tt in range(NT):
                        rstd, murstd, c_st = ln_stats(pout[:, tt, :], junk2[:, :], col=tt % 2,
                                                      src_ready=[rget2(("pt", tt))])
                        cV = kb.vop("tensor_scalar", [pj_t[:, :], pout[:, tt, :], rstd, murstd],
                                    kwargs={"op0": ALU.mult, "op1": ALU.subtract},
                                    waits=[c_st, rget2("pj")])
                        cV2 = kb.vop("tensor_tensor", [pj_t[:, :], pj_t[:, :], go_rep[:, :]],
                                     kwargs={"op": ALU.mult}, waits=[cV])
                        cV3 = kb.vop("tensor_tensor", [R[:, tt, :], R[:, tt, :], pj_t[:, :]],
                                     kwargs={"op": ALU.add}, waits=[cV2])
                        rel2["pj"] = cV3
            kb.barrier()
            if maybe_stop(kind, "bar", R[:, :, :], NT * D, waits=[("mV", kb.cnt["mV"])]):
                return True

        def ffn_phase(tag):
            pes = ExitStack()
            with pes:
                xnf = pes.enter_context(nc.sbuf_tensor(f"xnf_{tag}", [P, NDC, TOWN], BF16))
                xg = [pes.enter_context(nc.sbuf_tensor(f"xg_{tag}{i}", [P, D], BF16)) for i in range(2)]
                junk = pes.enter_context(nc.sbuf_tensor(f"junk_{tag}", [P, D], BF16))
                g_blk = pes.enter_context(nc.sbuf_tensor(f"gblk_{tag}", [P, 2, TOWN], BF16))
                h_full = pes.enter_context(nc.sbuf_tensor(f"h_{tag}", [P, SNIC, TOWN], BF16))
                w13b = [pes.enter_context(nc.sbuf_tensor(f"w13_{tag}{i}", [P, NDC, 256], BF16))
                        for i in range(2)]
                w2b = [pes.enter_context(nc.sbuf_tensor(f"w2_{tag}{i}", [P, SNIC, 256], BF16))
                       for i in range(2)]
                pgu = [pes.enter_context(nc.psum_tensor(f"pgu_{tag}{i}", [P, 512], F32)) for i in range(4)]
                py = [pes.enter_context(nc.psum_tensor(f"py_{tag}{i}", [P, 512], F32)) for i in range(3)]
                ptr = pes.enter_context(nc.psum_tensor(f"ptr_{tag}", [P, 1024], BF16))

                rel = {}

                def rget(key):
                    return rel.get(key, NOW)

                for tt in range(NT):
                    par = tt % 2
                    c_xn = ln_to_xn(R[:, tt, :], xg[par][:, :], junk[:, :], col=par,
                                    out_free=[rget(("xg", par))])
                    for half in (0, 1):
                        ins = [xg[par][:, (8 * half + j) * P:(8 * half + j + 1) * P] for j in range(8)]
                        outs = [ptr[:, j * P:(j + 1) * P] for j in range(8)]
                        c_tr = kb.transpose_pack(outs, ins, ident_bf[:, :],
                                                 waits=[c_xn, rget("ptr")])
                        c_ev = kb.vop("tensor_copy",
                                      [xnf[:, 8 * half:8 * half + 8, tt * P:(tt + 1) * P], ptr[:, :]],
                                      waits=[c_tr])
                        rel["ptr"] = c_ev
                    rel[("xg", par)] = c_tr
                c_xnf = c_ev

                probs = gate_t[:, 0:16]
                top8 = gate_t[:, 16:24]
                mask = gate_t[:, 32:48]
                pm = gate_t[:, 48:64]
                nmax = sml[:, 18:19]
                tsum = sml[:, 19:20]
                rcp = sml[:, 20:21]
                for tt in range(NT):
                    bank = py[0]
                    pairs = [(xnf[:, dc, tt * P:(tt + 1) * P], gwT_sb[:, dc, :]) for dc in range(NDC)]
                    c_mm = kb.mmg(bank[:, 0:16], pairs, waits=[c_xnf, rget("pg")])
                    c_nm = kb.vop("tensor_reduce", [nmax, bank[:, 0:16], AX.X, ALU.max],
                                  kwargs={"negate": True}, waits=[c_mm, rget("gate")])
                    c_pr = kb.act(probs, bank[:, 0:16], AF.Exp, bias=nmax, waits=[c_nm])
                    rel["pg"] = c_pr
                    c_t8 = kb.vop("max", [top8, probs], waits=[c_pr])
                    c_mk = kb.vop("tensor_scalar", [mask, probs, gate_t[:, 21:22], None],
                                  kwargs={"op0": ALU.is_ge}, waits=[c_t8])
                    c_pm = kb.vop("scalar_tensor_tensor", [pm, probs, 1.0, mask],
                                  kwargs={"op0": ALU.mult, "op1": ALU.mult, "accum_out": tsum},
                                  waits=[c_mk])
                    c_rc = kb.vop("reciprocal", [rcp, tsum], waits=[c_pm])
                    c_c = kb.vop("tensor_scalar", [c_sb[:, tt, :], pm, rcp, None],
                                 kwargs={"op0": ALU.mult}, waits=[c_rc])
                    rel["gate"] = c_c
                c_gate = c_c
                rel[("py", 0)] = rel["pg"]

                def expert_cfg(e):
                    shared = e == NE
                    return {
                        "e": e,
                        "shared": shared,
                        "nic": SNIC if shared else NIC,
                        "fi": SFI if shared else FI,
                        "hpar": 0 if shared else (e % 2),
                        "w13src": ap["sw13T"] if shared else ap["w13T"][e],
                        "w2src": ap["sw2T"] if shared else ap["w2T"][e],
                    }

                gu_rot = [0]

                def emit_gu(cfg):
                    e, shared, nic, fi, hpar = (cfg["e"], cfg["shared"], cfg["nic"],
                                                cfg["fi"], cfg["hpar"])
                    hbase = 0 if shared else NIC * hpar
                    if shared:
                        kb.waits("t", [rget(("h", 0)), rget(("h", 1))])
                    nblk = (fi + 255) // 256
                    for j in range(nblk):
                        w = min(256, fi - 256 * j)
                        for is_g in (True, False):
                            c0 = 256 * j + (0 if is_g else fi)
                            par = 0 if is_g else 1
                            c_wd = kb.dma("g", w13b[par][:, :, 0:w],
                                          cfg["w13src"][:, c0:c0 + w].rearrange("(c p) o -> p c o", p=P),
                                          waits=[rget(("w13", par))], stream=f"w13{par}")
                            for oc in range(w // 128):
                                ic = 2 * j + oc
                                for th in (0, 1):
                                    bank = pgu[gu_rot[0] % 4]
                                    gu_rot[0] += 1
                                    pairs = [(w13b[par][:, dc, oc * P:(oc + 1) * P],
                                              xnf[:, dc, th * 512:(th + 1) * 512])
                                             for dc in range(NDC)]
                                    c_mm = kb.mmg(bank[:, :], pairs,
                                                  waits=[c_wd, c_xnf, rget(("pgu", (gu_rot[0] - 1) % 4))])
                                    if is_g:
                                        c_ev = kb.act(g_blk[:, oc, th * 512:(th + 1) * 512], bank[:, :],
                                                      AF.Silu, waits=[c_mm, rget("gblk")])
                                    else:
                                        c_ev = kb.vop("tensor_tensor",
                                                      [h_full[:, hbase + ic, th * 512:(th + 1) * 512],
                                                       bank[:, :], g_blk[:, oc, th * 512:(th + 1) * 512]],
                                                      kwargs={"op": ALU.mult},
                                                      waits=[c_mm, rget(("hfree", hpar))])
                                    rel[("pgu", (gu_rot[0] - 1) % 4)] = c_ev
                            rel[("w13", par)] = ("mT", kb.cnt["mT"])
                            if not is_g:
                                rel["gblk"] = c_ev
                    cfg["c_h"] = ("mV", kb.cnt["mV"])
                    cfg["c_h_s"] = ("mS", kb.cnt["mS"])

                def emit_y(cfg):
                    e, shared, nic, hpar = cfg["e"], cfg["shared"], cfg["nic"], cfg["hpar"]
                    hbase = 0 if shared else NIC * hpar
                    for ob in range(8):
                        par = ob % 2
                        c_w2 = kb.dma("g", w2b[par][:, 0:nic, :],
                                      cfg["w2src"][:, ob * 256:(ob + 1) * 256]
                                      .rearrange("(c p) o -> p c o", p=P),
                                      waits=[rget(("w2", par))], stream=f"w2{par}")
                        for tt in range(NT):
                            bank = py[(ob * NT + tt) % 3]
                            pairs = [(h_full[:, hbase + ic, tt * P:(tt + 1) * P], w2b[par][:, ic, :])
                                     for ic in range(nic)]
                            c_mm = kb.mmg(bank[:, 0:256], pairs,
                                          waits=[c_w2, cfg["c_h"], cfg["c_h_s"], c_gate,
                                                 rget(("py", (ob * NT + tt) % 3))])
                            if shared:
                                c_y = kb.vop("tensor_tensor",
                                             [R[:, tt, ob * 256:(ob + 1) * 256], bank[:, 0:256],
                                              R[:, tt, ob * 256:(ob + 1) * 256]],
                                             kwargs={"op": ALU.add}, waits=[c_mm])
                            else:
                                c_y = kb.vop("scalar_tensor_tensor",
                                             [R[:, tt, ob * 256:(ob + 1) * 256], bank[:, 0:256],
                                              c_sb[:, tt, e:e + 1], R[:, tt, ob * 256:(ob + 1) * 256]],
                                             kwargs={"op0": ALU.mult, "op1": ALU.add}, waits=[c_mm])
                            rel[("py", (ob * NT + tt) % 3)] = c_y
                        rel[("w2", par)] = ("mT", kb.cnt["mT"])
                    rel[("h", hpar)] = ("mT", kb.cnt["mT"])
                    if shared:
                        rel[("h", 0)] = rel[("h", 1)] = ("mT", kb.cnt["mT"])
                    rel[("hfree", hpar)] = rel[("h", hpar)]

                cfgs = [expert_cfg(e) for e in range(NE + 1)]
                for e in range(NE + 1):
                    emit_gu(cfgs[e])
                    emit_y(cfgs[e])
            kb.barrier()

        stopped = False
        if "a1" in phases:
            stopped = attn_phase("a")
        if not stopped and "f1" in phases:
            stopped = ffn_phase("f1")
        if not stopped and "a2" in phases:
            stopped = attn_phase("b")
        if not stopped and "f2" in phases:
            stopped = ffn_phase("f2")

        c_out = kb.dma("y", out_ap.rearrange("(a p) d -> p a d", p=P), R[:, :, :])
        kb.wait("y", "dY", c_out[1])

        with nc.Block() as blk:
            @blk.tensor
            def _(E):
                for f in kb.q["t"]:
                    f(E)

            @blk.vector
            def _(E):
                for f in kb.q["v"]:
                    f(E)

            @blk.scalar
            def _(E):
                for f in kb.q["s"]:
                    f(E)

            @blk.sync
            def _(E):
                for f in kb.q["y"]:
                    f(E)

            @blk.gpsimd
            def _(E):
                for f in kb.q["g"]:
                    f(E)
    return nc


_W_CACHE = {}


def _prep_shared(inputs, need_f=True):
    key = (id(inputs.get("L0_ffn_w13")), need_f)
    if _W_CACHE.get("key") == key:
        return _W_CACHE
    c = _W_CACHE
    c.clear()
    c["key"] = key
    c["projT"] = np.ascontiguousarray(inputs["L0_attn_a_weight_proj"].T.astype(np.float32))
    c["wkv"] = np.ascontiguousarray(inputs["L0_attn_b_Wkv"].astype(np.float32))
    c["wout"] = np.ascontiguousarray(inputs["L0_attn_b_Wout"].astype(np.float32))
    c["gn"] = np.ascontiguousarray(inputs["L0_attn_b_g_norm"].astype(np.float32))
    c["go"] = np.ascontiguousarray(inputs["L0_attn_b_g_out"].astype(np.float32))
    c["gwT"] = np.ascontiguousarray(inputs["L0_ffn_gate_w"].T.astype(np.float32))
    if need_f:
        c["w13T"] = np.ascontiguousarray(np.transpose(inputs["L0_ffn_w13"], (0, 2, 1)).astype(np.float32))
        c["w2T"] = np.ascontiguousarray(np.transpose(inputs["L0_ffn_w2"], (0, 2, 1)).astype(np.float32))
        c["sw13T"] = np.ascontiguousarray(inputs["L0_ffn_sw13"].T.astype(np.float32))
        c["sw2T"] = np.ascontiguousarray(inputs["L0_ffn_sw2"].T.astype(np.float32))
    c["ident"] = np.eye(P, dtype=np.float32)
    c["ones"] = np.ones((P, P), dtype=np.float32)
    masks = np.zeros((4, P, 512), dtype=np.float32)
    ql = np.arange(P)[:, None]
    kl = np.arange(512)[None, :]
    for j in range(4):
        masks[j] = np.where(kl <= ql + 128 * j, 0.0, -30000.0).astype(np.float32)
    c["masks"] = masks
    c["bvals"] = np.array([[1e-5, 1e-12, -4.0, -16.0, math.log(4.0), math.log(64.0), 0.0, 0.0]],
                          dtype=np.float32)
    return c


def _in_maps(inputs, phases):
    need_f = ("f1" in phases) or ("f2" in phases)
    c = _prep_shared(inputs, need_f)
    x = np.ascontiguousarray(inputs["L0_attn_a_x"].astype(np.float32))
    wqkv = inputs["L0_attn_a_weight_qkv"].astype(np.float32)
    wq_all = inputs["L0_attn_b_Wq"].astype(np.float32)
    nonce = np.random.randint(1, 2 ** 31 - 10, dtype=np.uint32)
    fvals = (nonce + 1 + np.arange(8, dtype=np.uint32)).reshape(1, 8).astype(np.uint32)
    maps = []
    for core in range(8):
        b, h = divmod(core, 2)
        rows = []
        for i in range(8):
            hg = h * 8 + i
            g, j = divmod(hg, 4)
            rows.append(wqkv[g * 768 + j * 128: g * 768 + (j + 1) * 128])
        for g in (2 * h, 2 * h + 1):
            rows.append(wqkv[g * 768 + 512: g * 768 + 640])
        for g in (2 * h, 2 * h + 1):
            rows.append(wqkv[g * 768 + 640: g * 768 + 768])
        qkvT = np.ascontiguousarray(np.concatenate(rows, axis=0).T)
        wq = np.ascontiguousarray(wq_all[:, h * 1024:(h + 1) * 1024])
        m = {
            "x_own": np.ascontiguousarray(x[b, h * TOWN:(h + 1) * TOWN]),
            "x_full": np.ascontiguousarray(x[b]),
            "qkvT": qkvT,
            "projT": c["projT"],
            "wq": wq,
            "wkv": c["wkv"],
            "wout": c["wout"],
            "gn": c["gn"],
            "go": c["go"],
            "gwT": c["gwT"],
            **({"w13T": c["w13T"], "w2T": c["w2T"],
                "sw13T": c["sw13T"], "sw2T": c["sw2T"]} if need_f else {}),
            "ident": c["ident"],
            "ones": c["ones"],
            "masks": c["masks"],
            "fvals": fvals,
            "bvals": c["bvals"],
        }
        maps.append(m)
    return maps


_NC_CACHE = {}


def kernel(_phases=("a1", "f1", "a2", "f2"), **inputs):
    maps = _in_maps(inputs, _phases)
    key = (tuple(_phases), DBG_STOP)
    if key not in _NC_CACHE:
        _NC_CACHE[key] = build_program(_phases)
    nc = _NC_CACHE[key]
    res = run_bass_kernel_spmd(nc, maps, core_ids=list(range(8)))
    out = np.empty((4, SG, D), dtype=np.float32)
    for core in range(8):
        b, h = divmod(core, 2)
        out[b, h * TOWN:(h + 1) * TOWN] = res.results[core]["out"]
    if DBG_STOP is not None:
        kernel.dbg = [res.results[c]["dbg"] for c in range(8)]
    return out
